# version 04
# baseline (speedup 1.0000x reference)
import numpy as np
import ml_dtypes
import concourse.bass as bass
import concourse.mybir as mybir
from concourse.bass_utils import run_bass_kernel_spmd
from contextlib import ExitStack

F32 = mybir.dt.float32
BF16 = mybir.dt.bfloat16
AF = mybir.ActivationFunctionType
ALU = mybir.AluOpType
AX = mybir.AxisListType

TOK = 2048
NT = 16
NTG = 4
NCH = 16
H = 1024
DM = 256
KC = 8
NE = 9
NQ = 4
FQ = 8
EPS = 1e-6


class Em:
    ENGS = ("tensor", "scalar", "vector", "gpsimd", "sync")

    def __init__(self, nc, ctx):
        self.nc = nc
        self.ops = {e: [] for e in self.ENGS}
        self.cnt = {}
        self.sem = {}
        self.ctx = ctx
        self.last_wait = {}

    def new_sem(self, name):
        self.sem[name] = self.ctx.enter_context(self.nc.semaphore(name))
        self.cnt[name] = 0

    SELF_SEM = {"scalar": "ac", "vector": "dv"}

    def op(self, eng, fn, waits=(), inc=None, amt=1):
        pruned = []
        ss = self.SELF_SEM.get(eng)
        if ss is not None and self.cnt.get(ss, 0) > 0:
            waits = list(waits) + [(ss, self.cnt[ss])]
        for s, v in waits:
            if v is None or v <= 0:
                continue
            k = (eng, s)
            if v > self.last_wait.get(k, 0):
                pruned.append((s, v))
                self.last_wait[k] = v
        if inc is not None:
            self.cnt[inc] += amt
        self.ops[eng].append((pruned, fn, inc, amt))
        return self.cnt[inc] if inc else None

    def play(self):
        block = self.ctx.enter_context(self.nc.Block())
        sems = self.sem

        def mk(eng_name):
            def body(eng):
                for pruned, fn, inc, amt in self.ops[eng_name]:
                    for s, v in pruned:
                        eng.wait_ge(sems[s], v)
                    ins = fn(eng)
                    if inc is not None:
                        ins.then_inc(sems[inc], amt)
            return body
        for e in self.ENGS:
            if self.ops[e]:
                getattr(block, e)(mk(e))


class Banks:

    def __init__(self):
        self.w = {i: 0 for i in range(8)}
        self.r = {i: {} for i in range(8)}

    def write_waits(self, i):
        return list(self.r[i].items())

    def wrote(self, i, pe_cnt):
        self.w[i] = pe_cnt
        self.r[i] = {}

    def read_wait(self, i):
        return [("pe", self.w[i])]

    def read(self, i, sem, cnt):
        self.r[i][sem] = cnt


def build_program():
    nc = bass.Bass("TRN2", target_bir_lowering=False)
    dt = nc.dram_tensor
    x_own = dt("x_own", [TOK, H], F32, kind="ExternalInput").ap()
    x_pred = dt("x_pred", [TOK, H], F32, kind="ExternalInput").ap()
    cpred = dt("cpred", [1, 1], F32, kind="ExternalInput").ap()
    gate_g = dt("gate_g", [1, 1], F32, kind="ExternalInput").ap()
    wqT = dt("wqT", [H, DM], F32, kind="ExternalInput").ap()
    wkT = dt("wkT", [H, DM], F32, kind="ExternalInput").ap()
    wvT = dt("wvT", [H, DM], F32, kind="ExternalInput").ap()
    woT = dt("woT", [DM, H], F32, kind="ExternalInput").ap()
    memw = dt("memw", [DM, DM], F32, kind="ExternalInput").ap()
    memn = dt("memn", [DM], F32, kind="ExternalInput").ap()
    su = dt("su", [H, 4096], F32, kind="ExternalInput").ap()
    sd = dt("sd", [4096, H], F32, kind="ExternalInput").ap()
    ru = dt("ru", [8 * H, 4096], F32, kind="ExternalInput").ap()
    rd = dt("rd", [8 * 4096, H], F32, kind="ExternalInput").ap()
    rw = dt("rw", [H, 8], F32, kind="ExternalInput").ap()
    maskc = dt("maskc", [128, 128], BF16, kind="ExternalInput").ap()
    identc = dt("identc", [128, 128], F32, kind="ExternalInput").ap()
    onesrow = dt("onesrow", [1, 128], F32, kind="ExternalInput").ap()
    onecol = dt("onecol", [128, 1], BF16, kind="ExternalInput").ap()
    out = dt("out", [TOK, H], F32, kind="ExternalOutput").ap()

    with ExitStack() as ctx:
        sb = lambda n, s, d: ctx.enter_context(nc.sbuf_tensor(n, s, d))
        acc = sb("acc", [128, NT * H], F32)
        xnfm = sb("xnfm", [128, KC * TOK], BF16)
        u_bf = sb("u_bf", [128, 2 * 8192], BF16)
        d_bf = sb("d_bf", [128, 2 * 8192], BF16)
        ht_sb = sb("ht_sb", [128, FQ * 512], BF16)
        emin = sb("emin", [128, 768], F32)
        eexp = sb("eexp", [128, 1024], F32)
        sqf = sb("sqf", [128, 2 * 512], BF16)
        skf = sb("skf", [128, 2 * 512], BF16)
        ktm = sb("ktm", [128, 2 * 256], BF16)
        vtm = sb("vtm", [128, 2 * 256], BF16)
        s_sb = sb("s_sb", [128, 512], BF16)
        z_f = sb("z_f", [128, 2], F32)
        z_bf = sb("z_bf", [128, 2], BF16)
        comb_l = sb("comb_l", [128, 256], F32)
        comb_m = sb("comb_m", [128, 256], F32)
        cfm = sb("cfm", [128, 256], BF16)
        atm = sb("atm", [128, 128], BF16)
        rr = sb("rr", [128, 4], F32)
        eps_t = sb("eps_t", [128, 1], F32)
        stats = sb("stats", [128, NT * 8], F32)
        prs = sb("prs", [128, 2 * 16], F32)
        rstat = sb("rstat", [128, 2 * 8], F32)
        wqT_b = sb("wqT_b", [128, KC * 256], BF16)
        wkT_b = sb("wkT_b", [128, KC * 256], BF16)
        wvT_b = sb("wvT_b", [128, KC * 256], BF16)
        woT_b = sb("woT_b", [128, 2 * 1024], BF16)
        mem_b = sb("mem_b", [128, 2 * 256], BF16)
        memn_b = sb("memn_b", [128, 2], BF16)
        rw_b = sb("rw_b", [128, KC * 8], BF16)
        memn_r = sb("memn_r", [1, 256], F32)
        gate_s = sb("gate_s", [1, 1], F32)
        cp_s = sb("cp_s", [1, 1], F32)
        ones_r = sb("ones_r", [1, 128], F32)
        onec_b = sb("onec_b", [128, 1], BF16)
        mask_b = sb("mask_b", [128, 128], BF16)
        ident_f = sb("ident_f", [128, 128], F32)
        coef_r = sb("coef_r", [1, 4], F32)
        coef = sb("coef", [128, 4], F32)
        oscr = sb("oscr", [1, 8], F32)
        gates = sb("gates", [128, NT * NE], F32)
        P = [ctx.enter_context(nc.psum_tensor(f"P{i}", [128, 512], F32)) for i in range(8)]

        em = Em(nc, ctx)
        for s in ("ld", "wa", "wda", "wdb", "pe", "ac", "dv", "st"):
            em.new_sem(s)
        bk = Banks()
        EO = em.op

        acc3 = acc[:].rearrange("p (i c) -> p i c", i=NT)
        EO("sync", lambda e: e.dma_start(out=acc3, in_=x_pred.rearrange("(i p) c -> p i c", p=128)), inc="ld", amt=16)
        ld_predx = em.cnt["ld"]
        for dst, src in ((gate_s, gate_g), (cp_s, cpred), (ones_r, onesrow),
                         (ident_f, identc), (memn_r, memn.rearrange("(a c) -> a c", a=1)),
                         (mask_b, maskc), (onec_b, onecol)):
            EO("sync", lambda e, dst=dst, src=src: e.dma_start(out=dst[:], in_=src), inc="ld", amt=16)
        ld_consts = em.cnt["ld"]

        def cast3(dst_t, dst_groups, src_ap, p=128):
            dst = dst_t[:].rearrange("p (k c) -> p k c", k=dst_groups)
            src = src_ap.rearrange("(k p) c -> p k c", p=p)
            return lambda e: e.dma_start(out=dst, in_=src)
        EO("gpsimd", cast3(wqT_b, KC, wqT), inc="wa", amt=16)
        EO("gpsimd", cast3(wkT_b, KC, wkT), inc="wa", amt=16)
        EO("gpsimd", cast3(wvT_b, KC, wvT), inc="wa", amt=16)
        EO("gpsimd", cast3(woT_b, 2, woT), inc="wa", amt=16)
        EO("gpsimd", cast3(mem_b, 2, memw), inc="wa", amt=16)
        EO("gpsimd", cast3(rw_b, KC, rw), inc="wa", amt=16)
        EO("gpsimd", lambda e: e.dma_start(
            out=memn_b[:], in_=memn.rearrange("(k p) -> p k", p=128),
            allow_slow_non_contiguous=True), inc="wa", amt=16)
        wd_attn = ("wa", em.cnt["wa"])

        ru3 = ru.rearrange("(e k p) c -> e p k c", e=8, p=128)
        rd3 = rd.rearrange("(e f p) c -> e p f c", e=8, p=128)
        su3 = su.rearrange("(k p) c -> p k c", p=128)
        sd3 = sd.rearrange("(f p) c -> p f c", p=128)

        def emit_wdma(qi, pe_wait):
            e_, q = divmod(qi, NQ)
            b = qi % 2
            u_dst = u_bf[:, b * 8192:(b + 1) * 8192].rearrange("p (k c) -> p k c", k=KC)
            d_dst = d_bf[:, b * 8192:(b + 1) * 8192].rearrange("p (f c) -> p f c", f=FQ)
            if e_ == 0:
                u_src = su3[:, :, q * 1024:(q + 1) * 1024]
                d_src = sd3[:, q * FQ:(q + 1) * FQ, :]
            else:
                u_src = ru3[e_ - 1][:, :, q * 1024:(q + 1) * 1024]
                d_src = rd3[e_ - 1][:, q * FQ:(q + 1) * FQ, :]
            sname = "wda" if b == 0 else "wdb"
            EO("gpsimd", lambda e: e.dma_start(out=u_dst, in_=u_src),
               waits=[("pe", pe_wait)], inc=sname, amt=16)
            return (sname, EO("gpsimd", lambda e: e.dma_start(out=d_dst, in_=d_src), inc=sname, amt=16))

        wd_q = {}
        wd_q[0] = emit_wdma(0, 0)
        wd_q[1] = emit_wdma(1, 0)

        EO("vector", lambda e: e.memset(z_f[:], 0.0), inc="dv")
        EO("vector", lambda e: e.memset(eps_t[:], 1e-5), inc="dv")
        EO("vector", lambda e: e.memset(gates[:], 1.0), inc="dv")

        EO("scalar", lambda e: e.activation(oscr[:, 0:1], gate_s[:], AF.Exp, bias=0.0, scale=-1.0),
           waits=[("ld", ld_consts)], inc="ac")
        a_g = em.cnt["ac"]
        EO("vector", lambda e: e.tensor_scalar_add(oscr[:, 1:2], oscr[:, 0:1], 1.0),
           waits=[("ac", a_g)], inc="dv")
        EO("vector", lambda e: e.reciprocal(oscr[:, 2:3], oscr[:, 1:2]), inc="dv")
        EO("vector", lambda e: e.tensor_reduce(oscr[:, 3:4], memn_r[:], AX.X, ALU.add), inc="dv")
        EO("vector", lambda e: e.tensor_single_scalar(oscr[:, 4:5], oscr[:, 3:4], EPS, ALU.is_ge), inc="dv")
        EO("vector", lambda e: e.tensor_mul(coef_r[:, 1:2], oscr[:, 2:3], oscr[:, 4:5]), inc="dv")
        EO("vector", lambda e: e.tensor_scalar(coef_r[:, 2:3], oscr[:, 2:3], -1.0, 1.0, ALU.mult, ALU.add), inc="dv")
        EO("vector", lambda e: e.tensor_copy(coef_r[:, 0:1], cp_s[:]), inc="dv")
        d_coefr = em.cnt["dv"]
        n = EO("tensor", lambda e: e.matmul(P[4][:, 0:3], ones_r[:], coef_r[:, 0:3], start=True, stop=True),
               waits=[("dv", d_coefr), ("ld", ld_consts)] + bk.write_waits(4), inc="pe")
        bk.wrote(4, n)
        d_coef = EO("vector", lambda e: e.tensor_copy(coef[:, 0:3], P[4][:, 0:3]),
                    waits=bk.read_wait(4), inc="dv")
        bk.read(4, "dv", d_coef)

        xnfm4 = xnfm[:].rearrange("p (k t) -> p k t", k=KC)
        ln_prev_tp = [0]

        def emit_ln(i, own, extra_waits=()):
            so = i * 8
            src = acc[:, i * H:(i + 1) * H]
            w0 = list(extra_waits)
            EO("vector", lambda e: e.tensor_reduce(stats[:, so:so + 1], src, AX.X, ALU.add, negate=True),
               waits=w0, inc="dv")
            EO("scalar", lambda e: e.activation(eexp[:], src, AF.Square),
               waits=w0 + [("pe", ln_prev_tp[0])], inc="ac")
            a_sq = em.cnt["ac"]
            EO("vector", lambda e: e.tensor_reduce(stats[:, so + 2:so + 3], eexp[:], AX.X, ALU.add),
               waits=[("ac", a_sq)], inc="dv")
            EO("vector", lambda e: e.tensor_scalar_mul(stats[:, so + 1:so + 2], stats[:, so:so + 1], 1.0 / H), inc="dv")
            EO("vector", lambda e: e.tensor_mul(stats[:, so + 3:so + 4], stats[:, so + 1:so + 2],
                                                stats[:, so + 1:so + 2]), inc="dv")
            EO("vector", lambda e: e.tensor_scalar(stats[:, so + 4:so + 5], stats[:, so + 2:so + 3],
                                                   1.0 / H, stats[:, so + 3:so + 4], ALU.mult, ALU.subtract), inc="dv")
            d_var = em.cnt["dv"]
            EO("scalar", lambda e: e.activation(stats[:, so + 5:so + 6], stats[:, so + 4:so + 5], AF.Ln, bias=eps_t[:]),
               waits=[("dv", d_var)], inc="ac")
            EO("scalar", lambda e: e.activation(stats[:, so + 6:so + 7], stats[:, so + 5:so + 6], AF.Exp,
                                                scale=-0.5), inc="ac")
            a_rs = em.cnt["ac"]
            EO("vector", lambda e: e.tensor_scalar(eexp[:], src, stats[:, so + 1:so + 2],
                                                   stats[:, so + 6:so + 7], ALU.add, ALU.mult),
               waits=[("ac", a_rs), ("pe", ln_prev_tp[0])], inc="dv")
            d_xn = em.cnt["dv"]
            if own:
                EO("vector", lambda e: e.tensor_add(src, src, eexp[:]), inc="dv")
            for half in range(2):
                for j in range(4):
                    k = half * 4 + j
                    w = [("dv", d_xn), ("ld", ld_consts)] + (bk.write_waits(7) if j == 0 else [])
                    n = EO("tensor", lambda e, k=k, j=j: e.transpose(
                        P[7][:, j * 128:(j + 1) * 128], eexp[:, k * 128:(k + 1) * 128], ident_f[:]),
                        waits=w, inc="pe" if j == 3 else None)
                bk.wrote(7, n)
                dst = xnfm4[:, half * 4:(half + 1) * 4, i * 128:(i + 1) * 128]
                srcp = P[7][:].rearrange("p (j c) -> p j c", j=4)
                d_ev = EO("vector", lambda e, dst=dst, srcp=srcp: e.tensor_copy(dst, srcp),
                          waits=bk.read_wait(7), inc="dv")
                bk.read(7, "dv", d_ev)
            ln_prev_tp[0] = n
            return d_ev

        for i in range(NT):
            emit_ln(i, own=False, extra_waits=[("ld", ld_consts)])
        d_predln = em.cnt["dv"]
        a_predln = em.cnt["ac"]
        pe_predln = em.cnt["pe"]

        def emit_kv(ci, pred, guard_ac, guard_pe):
            par = ci % 2
            bank = par
            er = 512
            w = bk.write_waits(bank) + [wd_attn, ("dv", d_coef)]
            for k in range(KC):
                EO("tensor", lambda e, k=k, par=par, ci=ci: e.matmul(
                    P[par][:, 0:256],
                    xnfm[:, k * TOK + ci * 128: k * TOK + (ci + 1) * 128],
                    wkT_b[:, k * 256:(k + 1) * 256],
                    start=(k == 0), stop=(k == KC - 1)),
                    waits=w if k == 0 else (), inc=None)
            for k in range(KC):
                n = EO("tensor", lambda e, k=k, par=par, ci=ci: e.matmul(
                    P[par][:, 256:512],
                    xnfm[:, k * TOK + ci * 128: k * TOK + (ci + 1) * 128],
                    wvT_b[:, k * 256:(k + 1) * 256],
                    start=(k == 0), stop=(k == KC - 1)), inc="pe" if k == KC - 1 else None)
            bk.wrote(bank, n)
            rw_ = bk.read_wait(bank)
            EO("vector", lambda e, par=par, er=er: e.tensor_scalar_min(
                emin[:, er:er + 256], P[par][:, 0:256], 0.0),
                waits=rw_ + [("ac", guard_ac)], inc="dv")
            d_min = em.cnt["dv"]
            EO("scalar", lambda e, er=er: e.activation(
                eexp[:, er:er + 256], emin[:, er:er + 256], AF.Exp),
                waits=[("dv", d_min), ("pe", guard_pe)], inc="ac")
            a_exp = em.cnt["ac"]
            if pred:
                EO("vector", lambda e, par=par, er=er: e.scalar_tensor_tensor(
                    comb_m[:], P[par][:, 0:256], 0.0, eexp[:, er:er + 256], ALU.max, ALU.add),
                    waits=[("ac", a_exp)], inc="dv")
                EO("vector", lambda e, par=par: e.tensor_scalar_mul(
                    ktm[:, par * 256:(par + 1) * 256], comb_m[:], coef[:, 0:1]), inc="dv")
                d_v = EO("vector", lambda e, par=par: e.tensor_scalar_mul(
                    vtm[:, par * 256:(par + 1) * 256], P[par][:, 256:512], coef[:, 0:1]), inc="dv")
            else:
                EO("vector", lambda e, par=par, er=er: e.scalar_tensor_tensor(
                    ktm[:, par * 256:(par + 1) * 256], P[par][:, 0:256], 0.0,
                    eexp[:, er:er + 256], ALU.max, ALU.add), waits=[("ac", a_exp)], inc="dv")
                d_v = EO("vector", lambda e, par=par: e.tensor_copy(
                    vtm[:, par * 256:(par + 1) * 256], P[par][:, 256:512]), inc="dv")
            bk.read(bank, "dv", d_v)
            return d_v, d_v

        s_started = [False]

        def emit_state(ci, d_k, a_v, a_snap, last):
            par = ci % 2
            w = [("dv", max(d_k, a_v))]
            if a_snap is not None:
                w.append(("ac", a_snap))
            w += bk.write_waits(6)
            s_mm = None
            for dc in range(2):
                s_mm = EO("tensor", lambda e, dc=dc, par=par, st=s_started[0]: e.matmul(
                    P[6][:, dc * 256:(dc + 1) * 256],
                    ktm[:, par * 256 + dc * 128: par * 256 + (dc + 1) * 128],
                    vtm[:, par * 256:(par + 1) * 256],
                    start=(not st), stop=False, skip_group_check=True),
                    waits=w if dc == 0 else (), inc="pe" if dc == 1 else None)
                s_started[0] = True
            bk.wrote(6, s_mm)
            w4 = bk.write_waits(4) + [("ld", 16 * 8)]
            n = None
            for dc in range(2):
                n = EO("tensor", lambda e, dc=dc, par=par: e.matmul(
                    P[4][:, 136 + dc:137 + dc],
                    ktm[:, par * 256 + dc * 128: par * 256 + (dc + 1) * 128],
                    onec_b[:], start=True, stop=True),
                    waits=w4 if dc == 0 else (), inc="pe" if dc == 1 else None)
            bk.wrote(4, n)
            d_z = EO("vector", lambda e: e.tensor_add(z_f[:], z_f[:], P[4][:, 136:138]),
                     waits=bk.read_wait(4), inc="dv")
            bk.read(4, "dv", d_z)
            return s_mm, d_z

        s_mm, d_z = None, None
        for ci in range(NCH):
            d_k, a_v = emit_kv(ci, pred=True, guard_ac=a_predln, guard_pe=pe_predln)
            s_mm, d_z = emit_state(ci, d_k, a_v, None, last=False)
        pe_predproj = em.cnt["pe"]
        dv_pred = em.cnt["dv"]
        ac_pred = em.cnt["ac"]
        a_snap = EO("scalar", lambda e: e.activation(s_sb[:], P[6][:], AF.Copy),
                    waits=bk.read_wait(6) + [("pe", s_mm)], inc="ac")
        bk.read(6, "ac", a_snap)
        EO("scalar", lambda e: e.activation(z_bf[:], z_f[:], AF.Copy), waits=[("dv", d_z)], inc="ac")
        a_zsnap = em.cnt["ac"]

        EO("sync", lambda e: e.dma_start(out=acc3, in_=x_own.rearrange("(i p) c -> p i c", p=128)),
           waits=[("dv", d_predln), ("ac", a_predln), ("pe", pe_predln)], inc="ld", amt=16)
        ld_own = em.cnt["ld"]
        xn_tile_dv = []
        for i in range(NT):
            xn_tile_dv.append(emit_ln(i, own=True,
                                      extra_waits=[("ld", ld_own), ("pe", pe_predproj),
                                                   ("dv", dv_pred), ("ac", ac_pred)]))
        d_ownln = em.cnt["dv"]
        a_ownln = em.cnt["ac"]
        pe_ownln = em.cnt["pe"]

        for i in range(NT):
            po = (i % 2) * 16
            ro = (i % 2) * 8
            w = bk.write_waits(5) + [("dv", xn_tile_dv[i]), wd_attn]
            n = None
            for k in range(KC):
                n = EO("tensor", lambda e, k=k, i=i: e.matmul(
                    P[5][:, 0:8],
                    xnfm[:, k * TOK + i * 128: k * TOK + (i + 1) * 128],
                    rw_b[:, k * 8:(k + 1) * 8],
                    start=(k == 0), stop=(k == KC - 1)),
                    waits=w if k == 0 else (), inc="pe" if k == KC - 1 else None)
            bk.wrote(5, n)
            rwt = bk.read_wait(5)
            EO("vector", lambda e, ro=ro: e.tensor_reduce(rstat[:, ro:ro + 1], P[5][:, 0:8], AX.X, ALU.max),
               waits=rwt, inc="dv")
            d_sub = EO("vector", lambda e, po=po, ro=ro: e.tensor_single_scalar(
                prs[:, po:po + 8], P[5][:, 0:8], rstat[:, ro:ro + 1], ALU.subtract), inc="dv")
            bk.read(5, "dv", d_sub)
            EO("scalar", lambda e, po=po: e.activation(prs[:, po + 8:po + 16], prs[:, po:po + 8], AF.Exp),
               waits=[("dv", d_sub)], inc="ac")
            a_e = em.cnt["ac"]
            EO("vector", lambda e, po=po, ro=ro: e.tensor_reduce(rstat[:, ro + 1:ro + 2], prs[:, po + 8:po + 16],
                                                                 AX.X, ALU.add), waits=[("ac", a_e)], inc="dv")
            EO("vector", lambda e, ro=ro: e.reciprocal(rstat[:, ro + 2:ro + 3], rstat[:, ro + 1:ro + 2]), inc="dv")
            EO("vector", lambda e, po=po, ro=ro: e.tensor_scalar_mul(prs[:, po:po + 8], prs[:, po + 8:po + 16],
                                                                     rstat[:, ro + 2:ro + 3]), inc="dv")
            EO("vector", lambda e, po=po, ro=ro: e.tensor_reduce(rstat[:, ro + 3:ro + 4], prs[:, po:po + 8],
                                                                 AX.X, ALU.max), inc="dv")
            EO("vector", lambda e, po=po, ro=ro: e.tensor_single_scalar(
                prs[:, po + 8:po + 16], prs[:, po:po + 8], rstat[:, ro + 3:ro + 4], ALU.is_equal), inc="dv")
            EO("vector", lambda e, po=po: e.scalar_tensor_tensor(
                prs[:, po + 8:po + 16], prs[:, po + 8:po + 16], -1e9, prs[:, po:po + 8],
                ALU.mult, ALU.add), inc="dv")
            EO("vector", lambda e, po=po, ro=ro: e.tensor_reduce(rstat[:, ro + 4:ro + 5], prs[:, po + 8:po + 16],
                                                                 AX.X, ALU.max), inc="dv")
            EO("vector", lambda e, i=i, po=po, ro=ro: e.scalar_tensor_tensor(
                gates[:, i * NE + 1:(i + 1) * NE], prs[:, po:po + 8], rstat[:, ro + 4:ro + 5], prs[:, po:po + 8],
                ALU.is_ge, ALU.mult), inc="dv")
        d_gates = em.cnt["dv"]

        def emit_fm_proj(tg, wT_b, dst):
            outs = []
            for dc in range(2):
                bank = 2 + dc
                w = bk.write_waits(bank) + [("dv", d_ownln), wd_attn]
                for k in range(KC):
                    EO("tensor", lambda e, k=k, dc=dc, tg=tg, bank=bank, wT_b=wT_b: e.matmul(
                        P[bank][:, 0:512],
                        wT_b[:, k * 256 + dc * 128: k * 256 + (dc + 1) * 128],
                        xnfm[:, k * TOK + tg * 512:k * TOK + (tg + 1) * 512],
                        start=(k == 0), stop=(k == KC - 1)),
                        waits=w if k == 0 else (), inc="pe" if k == KC - 1 else None)
                bk.wrote(bank, em.cnt["pe"])
                rwt = bk.read_wait(bank)
                EO("vector", lambda e, bank=bank: e.tensor_scalar_min(emin[:, 0:512], P[bank][:, 0:512], 0.0),
                   waits=rwt + [("ac", a_ownln)], inc="dv")
                d_min = em.cnt["dv"]
                EO("scalar", lambda e: e.activation(eexp[:, 0:512], emin[:, 0:512], AF.Exp),
                   waits=[("dv", d_min), ("pe", pe_ownln)], inc="ac")
                a_exp = em.cnt["ac"]
                d_s = EO("vector", lambda e, bank=bank, dc=dc, dst=dst: e.scalar_tensor_tensor(
                    dst[:, dc * 512:(dc + 1) * 512], P[bank][:, 0:512], 0.0,
                    eexp[:, 0:512], ALU.max, ALU.add), waits=[("ac", a_exp)], inc="dv")
                bk.read(bank, "dv", d_s)
                outs.append(d_s)
            return outs

        ao_evac = {}
        kv_ready = {}
        for tg in range(NTG):
            d_q = emit_fm_proj(tg, wqT_b, sqf)
            d_kf = emit_fm_proj(tg, wkT_b, skf)
            d_fm = max(d_q + d_kf)
            if tg == 0:
                kv_ready[0] = emit_kv(0, pred=False, guard_ac=a_ownln, guard_pe=pe_ownln)
            for cl in range(4):
                ci = tg * 4 + cl
                tl = cl * 128
                d_k, a_v = kv_ready[ci]
                w = bk.write_waits(4) + [("dv", d_fm)]
                n = None
                for dc in range(2):
                    n = EO("tensor", lambda e, dc=dc, tl=tl: e.matmul(
                        P[4][:, 0:128],
                        skf[:, dc * 512 + tl: dc * 512 + tl + 128],
                        sqf[:, dc * 512 + tl: dc * 512 + tl + 128],
                        start=(dc == 0), stop=(dc == 1)),
                        waits=w if dc == 0 else (), inc="pe" if dc == 1 else None)
                bk.wrote(4, n)
                d_atm = EO("vector", lambda e: e.tensor_tensor(atm[:], P[4][:, 0:128], mask_b[:], ALU.mult),
                           waits=bk.read_wait(4) + [("ld", ld_consts)], inc="dv")
                bk.read(4, "dv", d_atm)
                w = bk.write_waits(5) + [("dv", max(d_atm, a_v)), ("ac", a_snap)]
                EO("tensor", lambda e, ci=ci: e.matmul(
                    P[5][:, 0:256], atm[:], vtm[:, (ci % 2) * 256:((ci % 2) + 1) * 256],
                    start=True, stop=False), waits=w, inc=None)
                for dc in range(2):
                    EO("tensor", lambda e, dc=dc, tl=tl: e.matmul(
                        P[5][:, 0:256],
                        sqf[:, dc * 512 + tl: dc * 512 + tl + 128],
                        s_sb[:, dc * 256:(dc + 1) * 256],
                        start=False, stop=(dc == 1)), inc="pe" if dc == 1 else None)
                n = None
                for dc in range(2):
                    n = EO("tensor", lambda e, dc=dc, tl=tl: e.matmul(
                        P[5][:, 256:512],
                        sqf[:, dc * 512 + tl: dc * 512 + tl + 128],
                        mem_b[:, dc * 256:(dc + 1) * 256],
                        start=(dc == 0), stop=(dc == 1)), inc="pe" if dc == 1 else None)
                bk.wrote(5, n)
                w = bk.write_waits(4) + [("ac", a_zsnap), ("ld", ld_consts)]
                EO("tensor", lambda e: e.matmul(P[4][:, 128:129], atm[:], onec_b[:],
                                                start=True, stop=False), waits=w, inc=None)
                for dc in range(2):
                    EO("tensor", lambda e, dc=dc, tl=tl: e.matmul(
                        P[4][:, 128:129],
                        sqf[:, dc * 512 + tl: dc * 512 + tl + 128],
                        z_bf[:, dc:dc + 1], start=False, stop=(dc == 1)), inc="pe" if dc == 1 else None)
                n = None
                for dc in range(2):
                    n = EO("tensor", lambda e, dc=dc, tl=tl: e.matmul(
                        P[4][:, 132:133],
                        sqf[:, dc * 512 + tl: dc * 512 + tl + 128],
                        memn_b[:, dc:dc + 1], start=(dc == 0), stop=(dc == 1)), inc="pe" if dc == 1 else None)
                bk.wrote(4, n)
                if ci + 1 < NCH:
                    kv_ready[ci + 1] = emit_kv(ci + 1, pred=False, guard_ac=a_ownln, guard_pe=pe_ownln)
                rwt4 = bk.read_wait(4)
                rwt5 = bk.read_wait(5)
                EO("vector", lambda e: e.tensor_scalar_max(rr[:, 0:1], P[4][:, 128:129], EPS),
                   waits=rwt4, inc="dv")
                EO("vector", lambda e: e.tensor_scalar_max(rr[:, 1:2], P[4][:, 132:133], EPS), inc="dv")
                EO("vector", lambda e: e.reciprocal(rr[:, 2:3], rr[:, 0:1]), inc="dv")
                d_r = EO("vector", lambda e: e.reciprocal(rr[:, 3:4], rr[:, 1:2]), inc="dv")
                bk.read(4, "dv", d_r)
                EO("vector", lambda e: e.tensor_scalar(comb_l[:], P[5][:, 0:256], rr[:, 2:3],
                                                       coef[:, 2:3], ALU.mult, ALU.mult),
                   waits=rwt5 + [("dv", d_r)], inc="dv")
                d_cm = EO("vector", lambda e: e.tensor_scalar(comb_m[:], P[5][:, 256:512], rr[:, 3:4],
                                                              coef[:, 1:2], ALU.mult, ALU.mult), inc="dv")
                bk.read(5, "dv", d_cm)
                d_cf = EO("vector", lambda e: e.tensor_add(comb_l[:], comb_l[:], comb_m[:]), inc="dv")
                w = bk.write_waits(7) + [("dv", d_cf)]
                n = None
                for dc in range(2):
                    n = EO("tensor", lambda e, dc=dc: e.transpose(
                        P[7][:, dc * 128:(dc + 1) * 128], comb_l[:, dc * 128:(dc + 1) * 128], ident_f[:]),
                        waits=w if dc == 0 else (), inc="pe" if dc == 1 else None)
                bk.wrote(7, n)
                d_cfm = EO("vector", lambda e: e.tensor_copy(cfm[:], P[7][:, 0:256]),
                           waits=bk.read_wait(7), inc="dv")
                bk.read(7, "dv", d_cfm)
                d_ao = None
                for hh in range(2):
                    bank = 2 + hh
                    w = bk.write_waits(bank) + [("dv", d_cfm)]
                    for dc in range(2):
                        EO("tensor", lambda e, dc=dc, hh=hh, bank=bank: e.matmul(
                            P[bank][:, 0:512],
                            cfm[:, dc * 128:(dc + 1) * 128],
                            woT_b[:, dc * 1024 + hh * 512: dc * 1024 + (hh + 1) * 512],
                            start=(dc == 0), stop=(dc == 1)),
                            waits=w if dc == 0 else (), inc="pe" if dc == 1 else None)
                    bk.wrote(bank, em.cnt["pe"])
                    d_ao = EO("vector", lambda e, hh=hh, ci=ci, bank=bank: e.tensor_add(
                        acc[:, ci * H + hh * 512: ci * H + (hh + 1) * 512],
                        acc[:, ci * H + hh * 512: ci * H + (hh + 1) * 512],
                        P[bank][:, 0:512]), waits=bk.read_wait(bank), inc="dv")
                    bk.read(bank, "dv", d_ao)
                ao_evac[ci] = d_ao
                s_mm, d_z = emit_state(ci, d_k, a_v, a_snap, last=(ci == NCH - 1))
                if ci + 1 < NCH:
                    a_snap = EO("scalar", lambda e: e.activation(s_sb[:], P[6][:], AF.Copy),
                                waits=bk.read_wait(6) + [("pe", s_mm)], inc="ac")
                    bk.read(6, "ac", a_snap)
                    EO("scalar", lambda e: e.activation(z_bf[:], z_f[:], AF.Copy),
                       waits=[("dv", d_z)], inc="ac")
                    a_zsnap = em.cnt["ac"]

        ht_gelu = {}
        tcount = 0
        pe_group_end = {}
        for e_i in range(NE):
            for q in range(NQ):
                qi = e_i * NQ + q
                b = qi % 2
                for tg in range(NTG):
                    for f in range(FQ):
                        hb = f % 2
                        w = [wd_q[qi], ("dv", d_ownln)] + bk.write_waits(hb)
                        for k in range(KC):
                            EO("tensor", lambda e, k=k, f=f, b=b, tg=tg, hb=hb: e.matmul(
                                P[hb][:, 0:512],
                                u_bf[:, b * 8192 + k * 1024 + f * 128: b * 8192 + k * 1024 + (f + 1) * 128],
                                xnfm[:, k * TOK + tg * 512: k * TOK + (tg + 1) * 512],
                                start=(k == 0), stop=(k == KC - 1)),
                                waits=w if k == 0 else (), inc="pe" if k == KC - 1 else None)
                        bk.wrote(hb, em.cnt["pe"])
                        a_g = EO("scalar", lambda e, f=f, hb=hb: e.activation(
                            ht_sb[:, f * 512:(f + 1) * 512], P[hb][:, 0:512], AF.Gelu),
                            waits=bk.read_wait(hb), inc="ac")
                        bk.read(hb, "ac", a_g)
                        ht_gelu[f] = a_g
                    for t in range(4):
                        ys = tcount % 3
                        tcount += 1
                        banks = (2 + 2 * ys, 3 + 2 * ys)
                        ti = tg * 4 + t
                        for hh in range(2):
                            bank = banks[hh]
                            w = bk.write_waits(bank)
                            for f in range(FQ):
                                EO("tensor", lambda e, f=f, t=t, hh=hh, b=b, bank=bank: e.matmul(
                                    P[bank][:, 0:512],
                                    ht_sb[:, f * 512 + t * 128: f * 512 + (t + 1) * 128],
                                    d_bf[:, b * 8192 + f * 1024 + hh * 512: b * 8192 + f * 1024 + (hh + 1) * 512],
                                    start=(f == 0), stop=(f == FQ - 1)),
                                    waits=(w if f == 0 else []) + [("ac", ht_gelu[f])],
                                    inc="pe" if f == FQ - 1 else None)
                            bk.wrote(bank, em.cnt["pe"])
                        pe_group_end[qi] = em.cnt["pe"]
                        for hh in range(2):
                            bank = banks[hh]
                            d_y = EO("vector", lambda e, ti=ti, hh=hh, e_i=e_i, bank=bank: e.scalar_tensor_tensor(
                                acc[:, ti * H + hh * 512: ti * H + (hh + 1) * 512],
                                P[bank][:, 0:512],
                                gates[:, ti * NE + e_i: ti * NE + e_i + 1],
                                acc[:, ti * H + hh * 512: ti * H + (hh + 1) * 512],
                                ALU.mult, ALU.add),
                                waits=bk.read_wait(bank) + [("dv", d_gates), ("dv", ao_evac[ti])], inc="dv")
                            bk.read(bank, "dv", d_y)
                if qi + 2 < NE * NQ:
                    wd_q[qi + 2] = emit_wdma(qi + 2, pe_group_end[qi])

        d_all = em.cnt["dv"]
        EO("sync", lambda e: e.dma_start(out=out.rearrange("(i p) c -> p i c", p=128), in_=acc3),
           waits=[("dv", d_all)], inc="st", amt=16)
        EO("sync", lambda e: e.wait_ge(em.sem["st"], em.cnt["st"]))
        em.play()
    return nc


def kernel(**inputs):
    f32 = np.float32
    x = np.ascontiguousarray(np.asarray(inputs["L0_attn_hidden_states"], f32).reshape(16384, H))
    wq = np.asarray(inputs["L0_attn_w_q"], f32)
    wk = np.asarray(inputs["L0_attn_w_k"], f32)
    wv = np.asarray(inputs["L0_attn_w_v"], f32)
    wo = np.asarray(inputs["L0_attn_w_o"], f32)
    base = {
        "gate_g": np.asarray(inputs["L0_attn_gate"], f32).reshape(1, 1),
        "wqT": np.ascontiguousarray(wq.T),
        "wkT": np.ascontiguousarray(wk.T),
        "wvT": np.ascontiguousarray(wv.T),
        "woT": np.ascontiguousarray(wo.T),
        "memw": np.ascontiguousarray(np.asarray(inputs["L0_attn_memory"], f32)),
        "memn": np.ascontiguousarray(np.asarray(inputs["L0_attn_memory_norm"], f32)),
        "su": np.ascontiguousarray(np.asarray(inputs["L0_ffn_shared_up"], f32).reshape(H, 4096)),
        "sd": np.ascontiguousarray(np.asarray(inputs["L0_ffn_shared_down"], f32).reshape(4096, H)),
        "ru": np.ascontiguousarray(np.asarray(inputs["L0_ffn_routed_up"], f32).reshape(8 * H, 4096)),
        "rd": np.ascontiguousarray(np.asarray(inputs["L0_ffn_routed_down"], f32).reshape(8 * 4096, H)),
        "rw": np.ascontiguousarray(np.asarray(inputs["L0_ffn_router_w"], f32)),
        "maskc": np.triu(np.ones((128, 128), f32)).astype(ml_dtypes.bfloat16),
        "identc": np.eye(128, dtype=f32),
        "onesrow": np.ones((1, 128), f32),
        "onecol": np.ones((128, 1), ml_dtypes.bfloat16),
    }
    in_maps = []
    for c in range(8):
        own = np.ascontiguousarray(x[c * TOK:(c + 1) * TOK])
        if c % 2 == 1:
            pred = np.ascontiguousarray(x[(c - 1) * TOK:c * TOK])
            cp = np.ones((1, 1), f32)
        else:
            pred = own
            cp = np.zeros((1, 1), f32)
        m = dict(base)
        m["x_own"] = own
        m["x_pred"] = pred
        m["cpred"] = cp
        in_maps.append(m)

    nc = build_program()
    res = run_bass_kernel_spmd(nc, in_maps, core_ids=list(range(8)))
    outs = [np.asarray(r["out"]) for r in res.results]
    return np.concatenate(outs, axis=0).reshape(4, 4096, H)


# version 05
# speedup vs baseline: 1.0392x; 1.0392x over previous
import numpy as np
import ml_dtypes
import concourse.bass as bass
import concourse.mybir as mybir
from concourse.bass_utils import run_bass_kernel_spmd
from contextlib import ExitStack

F32 = mybir.dt.float32
BF16 = mybir.dt.bfloat16
AF = mybir.ActivationFunctionType
ALU = mybir.AluOpType
AX = mybir.AxisListType

TOK = 2048
NT = 16
NTG = 4
NCH = 16
H = 1024
DM = 256
KC = 8
NE = 9
NQ = 4
FQ = 8
EPS = 1e-6


class Em:
    ENGS = ("tensor", "scalar", "vector", "gpsimd", "sync")

    def __init__(self, nc, ctx):
        self.nc = nc
        self.ops = {e: [] for e in self.ENGS}
        self.cnt = {}
        self.sem = {}
        self.ctx = ctx
        self.last_wait = {}

    def new_sem(self, name):
        self.sem[name] = self.ctx.enter_context(self.nc.semaphore(name))
        self.cnt[name] = 0

    SELF_SEM = {"scalar": "ac", "vector": "dv"}

    def op(self, eng, fn, waits=(), inc=None, amt=1):
        pruned = []
        ss = self.SELF_SEM.get(eng)
        if ss is not None and self.cnt.get(ss, 0) > 0:
            waits = list(waits) + [(ss, self.cnt[ss])]
        for s, v in waits:
            if v is None or v <= 0:
                continue
            k = (eng, s)
            if v > self.last_wait.get(k, 0):
                pruned.append((s, v))
                self.last_wait[k] = v
        if inc is not None:
            self.cnt[inc] += amt
        self.ops[eng].append((pruned, fn, inc, amt))
        return self.cnt[inc] if inc else None

    def play(self):
        block = self.ctx.enter_context(self.nc.Block())
        sems = self.sem

        def mk(eng_name):
            def body(eng):
                for pruned, fn, inc, amt in self.ops[eng_name]:
                    for s, v in pruned:
                        eng.wait_ge(sems[s], v)
                    ins = fn(eng)
                    if inc is not None:
                        ins.then_inc(sems[inc], amt)
            return body
        for e in self.ENGS:
            if self.ops[e]:
                getattr(block, e)(mk(e))


class Banks:

    def __init__(self):
        self.w = {i: 0 for i in range(8)}
        self.r = {i: {} for i in range(8)}

    def write_waits(self, i):
        return list(self.r[i].items())

    def wrote(self, i, pe_cnt):
        self.w[i] = pe_cnt
        self.r[i] = {}

    def read_wait(self, i):
        return [("pe", self.w[i])]

    def read(self, i, sem, cnt):
        self.r[i][sem] = cnt


def build_program():
    nc = bass.Bass("TRN2", target_bir_lowering=False)
    dt = nc.dram_tensor
    x_own = dt("x_own", [TOK, H], F32, kind="ExternalInput").ap()
    x_pred = dt("x_pred", [TOK, H], F32, kind="ExternalInput").ap()
    cpred = dt("cpred", [1, 1], F32, kind="ExternalInput").ap()
    gate_g = dt("gate_g", [1, 1], F32, kind="ExternalInput").ap()
    wqT = dt("wqT", [H, DM], F32, kind="ExternalInput").ap()
    wkT = dt("wkT", [H, DM], F32, kind="ExternalInput").ap()
    wvT = dt("wvT", [H, DM], F32, kind="ExternalInput").ap()
    woT = dt("woT", [DM, H], F32, kind="ExternalInput").ap()
    memw = dt("memw", [DM, DM], F32, kind="ExternalInput").ap()
    memn = dt("memn", [DM], F32, kind="ExternalInput").ap()
    su = dt("su", [H, 4096], F32, kind="ExternalInput").ap()
    sd = dt("sd", [4096, H], F32, kind="ExternalInput").ap()
    ru = dt("ru", [8 * H, 4096], F32, kind="ExternalInput").ap()
    rd = dt("rd", [8 * 4096, H], F32, kind="ExternalInput").ap()
    rw = dt("rw", [H, 8], F32, kind="ExternalInput").ap()
    maskc = dt("maskc", [128, 128], BF16, kind="ExternalInput").ap()
    identc = dt("identc", [128, 128], F32, kind="ExternalInput").ap()
    onesrow = dt("onesrow", [1, 128], F32, kind="ExternalInput").ap()
    onecol = dt("onecol", [128, 1], BF16, kind="ExternalInput").ap()
    out = dt("out", [TOK, H], F32, kind="ExternalOutput").ap()

    with ExitStack() as ctx:
        sb = lambda n, s, d: ctx.enter_context(nc.sbuf_tensor(n, s, d))
        acc = sb("acc", [128, NT * H], F32)
        xnfm = sb("xnfm", [128, KC * TOK], BF16)
        u_bf = sb("u_bf", [128, 2 * 8192], BF16)
        d_bf = sb("d_bf", [128, 2 * 8192], BF16)
        ht_sb = sb("ht_sb", [128, FQ * 512], BF16)
        emin = sb("emin", [128, 768], F32)
        eexp = sb("eexp", [128, 768], F32)
        lnx = sb("lnx", [128, 1024], F32)
        sqf = sb("sqf", [128, 2 * 512], BF16)
        skf = sb("skf", [128, 2 * 512], BF16)
        ktm = sb("ktm", [128, 2 * 256], BF16)
        vtm = sb("vtm", [128, 2 * 256], BF16)
        s_sb = sb("s_sb", [128, 512], BF16)
        z_f = sb("z_f", [128, 2], F32)
        z_bf = sb("z_bf", [128, 2], BF16)
        comb_l = sb("comb_l", [128, 256], F32)
        cfm = sb("cfm", [128, 256], BF16)
        atm = sb("atm", [128, 128], BF16)
        rr = sb("rr", [128, 4], F32)
        eps_t = sb("eps_t", [128, 1], F32)
        stats = sb("stats", [128, NT * 8], F32)
        prs = sb("prs", [128, 2 * 16], F32)
        rstat = sb("rstat", [128, 2 * 8], F32)
        wqT_b = sb("wqT_b", [128, KC * 256], BF16)
        wkT_b = sb("wkT_b", [128, KC * 256], BF16)
        wvT_b = sb("wvT_b", [128, KC * 256], BF16)
        woT_b = sb("woT_b", [128, 2 * 1024], BF16)
        mem_b = sb("mem_b", [128, 2 * 256], BF16)
        memn_b = sb("memn_b", [128, 2], BF16)
        rw_b = sb("rw_b", [128, KC * 8], BF16)
        gate_s = sb("gate_s", [1, 1], F32)
        cp_s = sb("cp_s", [1, 1], F32)
        onec_b = sb("onec_b", [128, 1], BF16)
        mask_b = sb("mask_b", [128, 128], BF16)
        ident_f = sb("ident_f", [128, 128], F32)
        coef_r = sb("coef_r", [1, 4], F32)
        memn_r = lnx[0:1, 0:256]
        ones_r = lnx[0:1, 512:640]
        coef = sb("coef", [128, 4], F32)
        oscr = sb("oscr", [1, 8], F32)
        gates = sb("gates", [128, NT * NE], F32)
        P = [ctx.enter_context(nc.psum_tensor(f"P{i}", [128, 512], F32)) for i in range(8)]

        em = Em(nc, ctx)
        for s in ("ld", "wa", "wda", "wdb", "pe", "ac", "dv", "st"):
            em.new_sem(s)
        bk = Banks()
        EO = em.op
        pe_coef = [0]
        dv_coef = [0]

        acc3 = acc[:].rearrange("p (i c) -> p i c", i=NT)
        EO("sync", lambda e: e.dma_start(out=acc3, in_=x_pred.rearrange("(i p) c -> p i c", p=128)), inc="ld", amt=16)
        ld_predx = em.cnt["ld"]
        for dst, src in ((gate_s[:], gate_g), (cp_s[:], cpred), (ones_r, onesrow),
                         (ident_f[:], identc), (memn_r, memn.rearrange("(a c) -> a c", a=1)),
                         (mask_b[:], maskc), (onec_b[:], onecol)):
            EO("sync", lambda e, dst=dst, src=src: e.dma_start(out=dst, in_=src), inc="ld", amt=16)
        ld_consts = em.cnt["ld"]

        def cast3(dst_t, dst_groups, src_ap, p=128):
            dst = dst_t[:].rearrange("p (k c) -> p k c", k=dst_groups)
            src = src_ap.rearrange("(k p) c -> p k c", p=p)
            return lambda e: e.dma_start(out=dst, in_=src)
        EO("gpsimd", cast3(wqT_b, KC, wqT), inc="wa", amt=16)
        EO("gpsimd", cast3(wkT_b, KC, wkT), inc="wa", amt=16)
        EO("gpsimd", cast3(wvT_b, KC, wvT), inc="wa", amt=16)
        EO("gpsimd", cast3(woT_b, 2, woT), inc="wa", amt=16)
        EO("gpsimd", cast3(mem_b, 2, memw), inc="wa", amt=16)
        EO("gpsimd", cast3(rw_b, KC, rw), inc="wa", amt=16)
        EO("gpsimd", lambda e: e.dma_start(
            out=memn_b[:], in_=memn.rearrange("(k p) -> p k", p=128),
            allow_slow_non_contiguous=True), inc="wa", amt=16)
        wd_attn = ("wa", em.cnt["wa"])

        ru3 = ru.rearrange("(e k p) c -> e p k c", e=8, p=128)
        rd3 = rd.rearrange("(e f p) c -> e p f c", e=8, p=128)
        su3 = su.rearrange("(k p) c -> p k c", p=128)
        sd3 = sd.rearrange("(f p) c -> p f c", p=128)

        def emit_wdma(qi, pe_wait):
            e_, q = divmod(qi, NQ)
            b = qi % 2
            u_dst = u_bf[:, b * 8192:(b + 1) * 8192].rearrange("p (k c) -> p k c", k=KC)
            d_dst = d_bf[:, b * 8192:(b + 1) * 8192].rearrange("p (f c) -> p f c", f=FQ)
            if e_ == 0:
                u_src = su3[:, :, q * 1024:(q + 1) * 1024]
                d_src = sd3[:, q * FQ:(q + 1) * FQ, :]
            else:
                u_src = ru3[e_ - 1][:, :, q * 1024:(q + 1) * 1024]
                d_src = rd3[e_ - 1][:, q * FQ:(q + 1) * FQ, :]
            sname = "wda" if b == 0 else "wdb"
            EO("gpsimd", lambda e: e.dma_start(out=u_dst, in_=u_src),
               waits=[("pe", pe_wait)], inc=sname, amt=16)
            return (sname, EO("gpsimd", lambda e: e.dma_start(out=d_dst, in_=d_src), inc=sname, amt=16))

        wd_q = {}
        wd_q[0] = emit_wdma(0, 0)
        wd_q[1] = emit_wdma(1, 0)

        EO("vector", lambda e: e.memset(z_f[:], 0.0), inc="dv")
        EO("vector", lambda e: e.memset(eps_t[:], 1e-5), inc="dv")
        EO("vector", lambda e: e.memset(gates[:], 1.0), inc="dv")

        EO("scalar", lambda e: e.activation(oscr[:, 0:1], gate_s[:], AF.Exp, bias=0.0, scale=-1.0),
           waits=[("ld", ld_consts)], inc="ac")
        a_g = em.cnt["ac"]
        EO("vector", lambda e: e.tensor_scalar_add(oscr[:, 1:2], oscr[:, 0:1], 1.0),
           waits=[("ac", a_g)], inc="dv")
        EO("vector", lambda e: e.reciprocal(oscr[:, 2:3], oscr[:, 1:2]), inc="dv")
        EO("vector", lambda e: e.tensor_reduce(oscr[:, 3:4], memn_r, AX.X, ALU.add), inc="dv")
        EO("vector", lambda e: e.tensor_single_scalar(oscr[:, 4:5], oscr[:, 3:4], EPS, ALU.is_ge), inc="dv")
        EO("vector", lambda e: e.tensor_mul(coef_r[:, 1:2], oscr[:, 2:3], oscr[:, 4:5]), inc="dv")
        EO("vector", lambda e: e.tensor_scalar(coef_r[:, 2:3], oscr[:, 2:3], -1.0, 1.0, ALU.mult, ALU.add), inc="dv")
        EO("vector", lambda e: e.tensor_copy(coef_r[:, 0:1], cp_s[:]), inc="dv")
        d_coefr = em.cnt["dv"]
        n = EO("tensor", lambda e: e.matmul(P[4][:, 0:3], ones_r, coef_r[:, 0:3], start=True, stop=True),
               waits=[("dv", d_coefr), ("ld", ld_consts)] + bk.write_waits(4), inc="pe")
        bk.wrote(4, n)
        d_coef = EO("vector", lambda e: e.tensor_copy(coef[:, 0:3], P[4][:, 0:3]),
                    waits=bk.read_wait(4), inc="dv")
        bk.read(4, "dv", d_coef)
        pe_coef[0] = em.cnt["pe"]
        dv_coef[0] = d_coef

        xnfm4 = xnfm[:].rearrange("p (k t) -> p k t", k=KC)
        ln_prev_tp = [0]

        def emit_ln(i, own, extra_waits=()):
            so = i * 8
            src = acc[:, i * H:(i + 1) * H]
            w0 = list(extra_waits)
            EO("vector", lambda e: e.tensor_reduce(stats[:, so:so + 1], src, AX.X, ALU.add, negate=True),
               waits=w0, inc="dv")
            EO("scalar", lambda e: e.activation(lnx[:], src, AF.Square),
               waits=w0 + [("pe", max(ln_prev_tp[0], pe_coef[0])), ("dv", dv_coef[0])], inc="ac")
            a_sq = em.cnt["ac"]
            EO("vector", lambda e: e.tensor_reduce(stats[:, so + 2:so + 3], lnx[:], AX.X, ALU.add),
               waits=[("ac", a_sq)], inc="dv")
            EO("vector", lambda e: e.tensor_scalar_mul(stats[:, so + 1:so + 2], stats[:, so:so + 1], 1.0 / H), inc="dv")
            EO("vector", lambda e: e.tensor_mul(stats[:, so + 3:so + 4], stats[:, so + 1:so + 2],
                                                stats[:, so + 1:so + 2]), inc="dv")
            EO("vector", lambda e: e.tensor_scalar(stats[:, so + 4:so + 5], stats[:, so + 2:so + 3],
                                                   1.0 / H, stats[:, so + 3:so + 4], ALU.mult, ALU.subtract), inc="dv")
            d_var = em.cnt["dv"]
            EO("scalar", lambda e: e.activation(stats[:, so + 5:so + 6], stats[:, so + 4:so + 5], AF.Ln, bias=eps_t[:]),
               waits=[("dv", d_var)], inc="ac")
            EO("scalar", lambda e: e.activation(stats[:, so + 6:so + 7], stats[:, so + 5:so + 6], AF.Exp,
                                                scale=-0.5), inc="ac")
            a_rs = em.cnt["ac"]
            EO("vector", lambda e: e.tensor_scalar(lnx[:], src, stats[:, so + 1:so + 2],
                                                   stats[:, so + 6:so + 7], ALU.add, ALU.mult),
               waits=[("ac", a_rs), ("pe", ln_prev_tp[0])], inc="dv")
            d_xn = em.cnt["dv"]
            if own:
                EO("vector", lambda e: e.tensor_add(src, src, lnx[:]), inc="dv")
            for half in range(2):
                for j in range(4):
                    k = half * 4 + j
                    w = [("dv", d_xn), ("ld", ld_consts)] + (bk.write_waits(7) if j == 0 else [])
                    n = EO("tensor", lambda e, k=k, j=j: e.transpose(
                        P[7][:, j * 128:(j + 1) * 128], lnx[:, k * 128:(k + 1) * 128], ident_f[:]),
                        waits=w, inc="pe" if j == 3 else None)
                bk.wrote(7, n)
                dst = xnfm4[:, half * 4:(half + 1) * 4, i * 128:(i + 1) * 128]
                srcp = P[7][:].rearrange("p (j c) -> p j c", j=4)
                d_ev = EO("vector", lambda e, dst=dst, srcp=srcp: e.tensor_copy(dst, srcp),
                          waits=bk.read_wait(7), inc="dv")
                bk.read(7, "dv", d_ev)
            ln_prev_tp[0] = n
            return d_ev

        predln_dv = []

        def emit_kv(ci, pred, tile_dv):
            par = ci % 2
            bank = par
            er = 512
            w = bk.write_waits(bank) + [wd_attn, ("dv", max(d_coef, tile_dv))]
            for k in range(KC):
                EO("tensor", lambda e, k=k, par=par, ci=ci: e.matmul(
                    P[par][:, 0:256],
                    xnfm[:, k * TOK + ci * 128: k * TOK + (ci + 1) * 128],
                    wkT_b[:, k * 256:(k + 1) * 256],
                    start=(k == 0), stop=(k == KC - 1)),
                    waits=w if k == 0 else (), inc=None)
            for k in range(KC):
                n = EO("tensor", lambda e, k=k, par=par, ci=ci: e.matmul(
                    P[par][:, 256:512],
                    xnfm[:, k * TOK + ci * 128: k * TOK + (ci + 1) * 128],
                    wvT_b[:, k * 256:(k + 1) * 256],
                    start=(k == 0), stop=(k == KC - 1)), inc="pe" if k == KC - 1 else None)
            bk.wrote(bank, n)
            rw_ = bk.read_wait(bank)
            EO("vector", lambda e, par=par, er=er: e.tensor_scalar_min(
                emin[:, er:er + 256], P[par][:, 0:256], 0.0),
                waits=rw_, inc="dv")
            d_min = em.cnt["dv"]
            EO("scalar", lambda e, er=er: e.activation(
                eexp[:, er:er + 256], emin[:, er:er + 256], AF.Exp),
                waits=[("dv", d_min)], inc="ac")
            a_exp = em.cnt["ac"]
            if pred:
                EO("vector", lambda e, par=par, er=er: e.scalar_tensor_tensor(
                    eexp[:, 0:256], P[par][:, 0:256], 0.0, eexp[:, er:er + 256], ALU.max, ALU.add),
                    waits=[("ac", a_exp)], inc="dv")
                EO("vector", lambda e, par=par: e.tensor_scalar_mul(
                    ktm[:, par * 256:(par + 1) * 256], eexp[:, 0:256], coef[:, 0:1]), inc="dv")
                d_v = EO("vector", lambda e, par=par: e.tensor_scalar_mul(
                    vtm[:, par * 256:(par + 1) * 256], P[par][:, 256:512], coef[:, 0:1]), inc="dv")
            else:
                EO("vector", lambda e, par=par, er=er: e.scalar_tensor_tensor(
                    ktm[:, par * 256:(par + 1) * 256], P[par][:, 0:256], 0.0,
                    eexp[:, er:er + 256], ALU.max, ALU.add), waits=[("ac", a_exp)], inc="dv")
                d_v = EO("vector", lambda e, par=par: e.tensor_copy(
                    vtm[:, par * 256:(par + 1) * 256], P[par][:, 256:512]), inc="dv")
            bk.read(bank, "dv", d_v)
            return d_v, d_v

        s_started = [False]

        def emit_state(ci, d_k, a_v, a_snap, last):
            par = ci % 2
            w = [("dv", max(d_k, a_v))]
            if a_snap is not None:
                w.append(("ac", a_snap))
            w += bk.write_waits(6)
            s_mm = None
            for dc in range(2):
                s_mm = EO("tensor", lambda e, dc=dc, par=par, st=s_started[0]: e.matmul(
                    P[6][:, dc * 256:(dc + 1) * 256],
                    ktm[:, par * 256 + dc * 128: par * 256 + (dc + 1) * 128],
                    vtm[:, par * 256:(par + 1) * 256],
                    start=(not st), stop=False, skip_group_check=True),
                    waits=w if dc == 0 else (), inc="pe" if dc == 1 else None)
                s_started[0] = True
            bk.wrote(6, s_mm)
            w4 = bk.write_waits(4) + [("ld", 16 * 8)]
            n = None
            for dc in range(2):
                n = EO("tensor", lambda e, dc=dc, par=par: e.matmul(
                    P[4][:, 136 + dc:137 + dc],
                    ktm[:, par * 256 + dc * 128: par * 256 + (dc + 1) * 128],
                    onec_b[:], start=True, stop=True),
                    waits=w4 if dc == 0 else (), inc="pe" if dc == 1 else None)
            bk.wrote(4, n)
            d_z = EO("vector", lambda e: e.tensor_add(z_f[:], z_f[:], P[4][:, 136:138]),
                     waits=bk.read_wait(4), inc="dv")
            bk.read(4, "dv", d_z)
            return s_mm, d_z

        s_mm, d_z = None, None
        for ci in range(NCH):
            predln_dv.append(emit_ln(ci, own=False, extra_waits=[("ld", ld_consts)]))
            d_k, a_v = emit_kv(ci, pred=True, tile_dv=predln_dv[ci])
            s_mm, d_z = emit_state(ci, d_k, a_v, None, last=False)
        d_predln = em.cnt["dv"]
        a_predln = em.cnt["ac"]
        pe_predln = em.cnt["pe"]
        pe_predproj = em.cnt["pe"]
        dv_pred = em.cnt["dv"]
        ac_pred = em.cnt["ac"]
        a_snap = EO("scalar", lambda e: e.activation(s_sb[:], P[6][:], AF.Copy),
                    waits=bk.read_wait(6) + [("pe", s_mm)], inc="ac")
        bk.read(6, "ac", a_snap)
        EO("scalar", lambda e: e.activation(z_bf[:], z_f[:], AF.Copy), waits=[("dv", d_z)], inc="ac")
        a_zsnap = em.cnt["ac"]

        EO("sync", lambda e: e.dma_start(out=acc3, in_=x_own.rearrange("(i p) c -> p i c", p=128)),
           waits=[("dv", d_predln), ("ac", a_predln), ("pe", pe_predln)], inc="ld", amt=16)
        ld_own = em.cnt["ld"]
        xn_tile_dv = []

        def emit_router(i):
            po = (i % 2) * 16
            ro = (i % 2) * 8
            w = bk.write_waits(5) + [("dv", xn_tile_dv[i]), wd_attn]
            n = None
            for k in range(KC):
                n = EO("tensor", lambda e, k=k, i=i: e.matmul(
                    P[5][:, 0:8],
                    xnfm[:, k * TOK + i * 128: k * TOK + (i + 1) * 128],
                    rw_b[:, k * 8:(k + 1) * 8],
                    start=(k == 0), stop=(k == KC - 1)),
                    waits=w if k == 0 else (), inc="pe" if k == KC - 1 else None)
            bk.wrote(5, n)
            rwt = bk.read_wait(5)
            EO("vector", lambda e, ro=ro: e.tensor_reduce(rstat[:, ro:ro + 1], P[5][:, 0:8], AX.X, ALU.max),
               waits=rwt, inc="dv")
            d_sub = EO("vector", lambda e, po=po, ro=ro: e.tensor_single_scalar(
                prs[:, po:po + 8], P[5][:, 0:8], rstat[:, ro:ro + 1], ALU.subtract), inc="dv")
            bk.read(5, "dv", d_sub)
            EO("scalar", lambda e, po=po: e.activation(prs[:, po + 8:po + 16], prs[:, po:po + 8], AF.Exp),
               waits=[("dv", d_sub)], inc="ac")
            a_e = em.cnt["ac"]
            EO("vector", lambda e, po=po, ro=ro: e.tensor_reduce(rstat[:, ro + 1:ro + 2], prs[:, po + 8:po + 16],
                                                                 AX.X, ALU.add), waits=[("ac", a_e)], inc="dv")
            EO("vector", lambda e, ro=ro: e.reciprocal(rstat[:, ro + 2:ro + 3], rstat[:, ro + 1:ro + 2]), inc="dv")
            EO("vector", lambda e, po=po, ro=ro: e.tensor_scalar_mul(prs[:, po:po + 8], prs[:, po + 8:po + 16],
                                                                     rstat[:, ro + 2:ro + 3]), inc="dv")
            EO("vector", lambda e, po=po, ro=ro: e.tensor_reduce(rstat[:, ro + 3:ro + 4], prs[:, po:po + 8],
                                                                 AX.X, ALU.max), inc="dv")
            EO("vector", lambda e, po=po, ro=ro: e.tensor_single_scalar(
                prs[:, po + 8:po + 16], prs[:, po:po + 8], rstat[:, ro + 3:ro + 4], ALU.is_equal), inc="dv")
            EO("vector", lambda e, po=po: e.scalar_tensor_tensor(
                prs[:, po + 8:po + 16], prs[:, po + 8:po + 16], -1e9, prs[:, po:po + 8],
                ALU.mult, ALU.add), inc="dv")
            EO("vector", lambda e, po=po, ro=ro: e.tensor_reduce(rstat[:, ro + 4:ro + 5], prs[:, po + 8:po + 16],
                                                                 AX.X, ALU.max), inc="dv")
            EO("vector", lambda e, i=i, po=po, ro=ro: e.scalar_tensor_tensor(
                gates[:, i * NE + 1:(i + 1) * NE], prs[:, po:po + 8], rstat[:, ro + 4:ro + 5], prs[:, po:po + 8],
                ALU.is_ge, ALU.mult), inc="dv")

        def emit_fm_proj(tg, wT_b, dst):
            outs = []
            for dc in range(2):
                bank = 2 + dc
                w = bk.write_waits(bank) + [("dv", xn_tile_dv[tg * 4 + 3]), wd_attn]
                for k in range(KC):
                    EO("tensor", lambda e, k=k, dc=dc, tg=tg, bank=bank, wT_b=wT_b: e.matmul(
                        P[bank][:, 0:512],
                        wT_b[:, k * 256 + dc * 128: k * 256 + (dc + 1) * 128],
                        xnfm[:, k * TOK + tg * 512:k * TOK + (tg + 1) * 512],
                        start=(k == 0), stop=(k == KC - 1)),
                        waits=w if k == 0 else (), inc="pe" if k == KC - 1 else None)
                bk.wrote(bank, em.cnt["pe"])
                rwt = bk.read_wait(bank)
                EO("vector", lambda e, bank=bank: e.tensor_scalar_min(emin[:, 0:512], P[bank][:, 0:512], 0.0),
                   waits=rwt, inc="dv")
                d_min = em.cnt["dv"]
                EO("scalar", lambda e: e.activation(eexp[:, 0:512], emin[:, 0:512], AF.Exp),
                   waits=[("dv", d_min)], inc="ac")
                a_exp = em.cnt["ac"]
                d_s = EO("vector", lambda e, bank=bank, dc=dc, dst=dst: e.scalar_tensor_tensor(
                    dst[:, dc * 512:(dc + 1) * 512], P[bank][:, 0:512], 0.0,
                    eexp[:, 0:512], ALU.max, ALU.add), waits=[("ac", a_exp)], inc="dv")
                bk.read(bank, "dv", d_s)
                outs.append(d_s)
            return outs

        ao_evac = {}
        kv_ready = {}
        ln_extra = [("ld", ld_own), ("pe", pe_predproj), ("dv", dv_pred), ("ac", ac_pred)]
        for tg in range(NTG):
            for i in range(tg * 4, tg * 4 + 4):
                xn_tile_dv.append(emit_ln(i, own=True, extra_waits=ln_extra))
                emit_router(i)
            d_q = emit_fm_proj(tg, wqT_b, sqf)
            d_kf = emit_fm_proj(tg, wkT_b, skf)
            d_fm = max(d_q + d_kf)
            kv_ready[tg * 4] = emit_kv(tg * 4, pred=False, tile_dv=xn_tile_dv[tg * 4])
            for cl in range(4):
                ci = tg * 4 + cl
                tl = cl * 128
                d_k, a_v = kv_ready[ci]
                w = bk.write_waits(4) + [("dv", d_fm)]
                n = None
                for dc in range(2):
                    n = EO("tensor", lambda e, dc=dc, tl=tl: e.matmul(
                        P[4][:, 0:128],
                        skf[:, dc * 512 + tl: dc * 512 + tl + 128],
                        sqf[:, dc * 512 + tl: dc * 512 + tl + 128],
                        start=(dc == 0), stop=(dc == 1)),
                        waits=w if dc == 0 else (), inc="pe" if dc == 1 else None)
                bk.wrote(4, n)
                d_atm = EO("vector", lambda e: e.tensor_tensor(atm[:], P[4][:, 0:128], mask_b[:], ALU.mult),
                           waits=bk.read_wait(4) + [("ld", ld_consts)], inc="dv")
                bk.read(4, "dv", d_atm)
                w = bk.write_waits(5) + [("dv", max(d_atm, a_v)), ("ac", a_snap)]
                EO("tensor", lambda e, ci=ci: e.matmul(
                    P[5][:, 0:256], atm[:], vtm[:, (ci % 2) * 256:((ci % 2) + 1) * 256],
                    start=True, stop=False), waits=w, inc=None)
                for dc in range(2):
                    EO("tensor", lambda e, dc=dc, tl=tl: e.matmul(
                        P[5][:, 0:256],
                        sqf[:, dc * 512 + tl: dc * 512 + tl + 128],
                        s_sb[:, dc * 256:(dc + 1) * 256],
                        start=False, stop=(dc == 1)), inc="pe" if dc == 1 else None)
                n = None
                for dc in range(2):
                    n = EO("tensor", lambda e, dc=dc, tl=tl: e.matmul(
                        P[5][:, 256:512],
                        sqf[:, dc * 512 + tl: dc * 512 + tl + 128],
                        mem_b[:, dc * 256:(dc + 1) * 256],
                        start=(dc == 0), stop=(dc == 1)), inc="pe" if dc == 1 else None)
                bk.wrote(5, n)
                w = bk.write_waits(4) + [("ac", a_zsnap), ("ld", ld_consts)]
                EO("tensor", lambda e: e.matmul(P[4][:, 128:129], atm[:], onec_b[:],
                                                start=True, stop=False), waits=w, inc=None)
                for dc in range(2):
                    EO("tensor", lambda e, dc=dc, tl=tl: e.matmul(
                        P[4][:, 128:129],
                        sqf[:, dc * 512 + tl: dc * 512 + tl + 128],
                        z_bf[:, dc:dc + 1], start=False, stop=(dc == 1)), inc="pe" if dc == 1 else None)
                n = None
                for dc in range(2):
                    n = EO("tensor", lambda e, dc=dc, tl=tl: e.matmul(
                        P[4][:, 132:133],
                        sqf[:, dc * 512 + tl: dc * 512 + tl + 128],
                        memn_b[:, dc:dc + 1], start=(dc == 0), stop=(dc == 1)), inc="pe" if dc == 1 else None)
                bk.wrote(4, n)
                if ci + 1 < NCH and ci + 1 <= tg * 4 + 3:
                    kv_ready[ci + 1] = emit_kv(ci + 1, pred=False, tile_dv=xn_tile_dv[ci + 1])
                rwt4 = bk.read_wait(4)
                rwt5 = bk.read_wait(5)
                EO("vector", lambda e: e.tensor_scalar_max(rr[:, 0:1], P[4][:, 128:129], EPS),
                   waits=rwt4, inc="dv")
                EO("vector", lambda e: e.tensor_scalar_max(rr[:, 1:2], P[4][:, 132:133], EPS), inc="dv")
                EO("vector", lambda e: e.reciprocal(rr[:, 2:3], rr[:, 0:1]), inc="dv")
                EO("vector", lambda e: e.reciprocal(rr[:, 3:4], rr[:, 1:2]), inc="dv")
                d_r = EO("vector", lambda e: e.tensor_mul(rr[:, 3:4], rr[:, 3:4], coef[:, 1:2]), inc="dv")
                bk.read(4, "dv", d_r)
                EO("vector", lambda e: e.tensor_scalar(comb_l[:], P[5][:, 0:256], rr[:, 2:3],
                                                       coef[:, 2:3], ALU.mult, ALU.mult),
                   waits=rwt5 + [("dv", d_r)], inc="dv")
                d_cf = EO("vector", lambda e: e.scalar_tensor_tensor(
                    comb_l[:], P[5][:, 256:512], rr[:, 3:4], comb_l[:], ALU.mult, ALU.add), inc="dv")
                d_cm = d_cf
                bk.read(5, "dv", d_cm)
                w = bk.write_waits(7) + [("dv", d_cf)]
                n = None
                for dc in range(2):
                    n = EO("tensor", lambda e, dc=dc: e.transpose(
                        P[7][:, dc * 128:(dc + 1) * 128], comb_l[:, dc * 128:(dc + 1) * 128], ident_f[:]),
                        waits=w if dc == 0 else (), inc="pe" if dc == 1 else None)
                bk.wrote(7, n)
                d_cfm = EO("vector", lambda e: e.tensor_copy(cfm[:], P[7][:, 0:256]),
                           waits=bk.read_wait(7), inc="dv")
                bk.read(7, "dv", d_cfm)
                d_ao = None
                for hh in range(2):
                    bank = 2 + hh
                    w = bk.write_waits(bank) + [("dv", d_cfm)]
                    for dc in range(2):
                        EO("tensor", lambda e, dc=dc, hh=hh, bank=bank: e.matmul(
                            P[bank][:, 0:512],
                            cfm[:, dc * 128:(dc + 1) * 128],
                            woT_b[:, dc * 1024 + hh * 512: dc * 1024 + (hh + 1) * 512],
                            start=(dc == 0), stop=(dc == 1)),
                            waits=w if dc == 0 else (), inc="pe" if dc == 1 else None)
                    bk.wrote(bank, em.cnt["pe"])
                    d_ao = EO("vector", lambda e, hh=hh, ci=ci, bank=bank: e.tensor_add(
                        acc[:, ci * H + hh * 512: ci * H + (hh + 1) * 512],
                        acc[:, ci * H + hh * 512: ci * H + (hh + 1) * 512],
                        P[bank][:, 0:512]), waits=bk.read_wait(bank), inc="dv")
                    bk.read(bank, "dv", d_ao)
                ao_evac[ci] = d_ao
                s_mm, d_z = emit_state(ci, d_k, a_v, a_snap, last=(ci == NCH - 1))
                if ci + 1 < NCH:
                    a_snap = EO("scalar", lambda e: e.activation(s_sb[:], P[6][:], AF.Copy),
                                waits=bk.read_wait(6) + [("pe", s_mm)], inc="ac")
                    bk.read(6, "ac", a_snap)
                    EO("scalar", lambda e: e.activation(z_bf[:], z_f[:], AF.Copy),
                       waits=[("dv", d_z)], inc="ac")
                    a_zsnap = em.cnt["ac"]

        d_ownln = em.cnt["dv"]
        d_gates = em.cnt["dv"]

        ht_gelu = {}
        tcount = 0
        pe_group_end = {}
        for e_i in range(NE):
            for q in range(NQ):
                qi = e_i * NQ + q
                b = qi % 2
                for tg in range(NTG):
                    for f in range(FQ):
                        hb = f % 2
                        w = [wd_q[qi], ("dv", d_ownln)] + bk.write_waits(hb)
                        for k in range(KC):
                            EO("tensor", lambda e, k=k, f=f, b=b, tg=tg, hb=hb: e.matmul(
                                P[hb][:, 0:512],
                                u_bf[:, b * 8192 + k * 1024 + f * 128: b * 8192 + k * 1024 + (f + 1) * 128],
                                xnfm[:, k * TOK + tg * 512: k * TOK + (tg + 1) * 512],
                                start=(k == 0), stop=(k == KC - 1)),
                                waits=w if k == 0 else (), inc="pe" if k == KC - 1 else None)
                        bk.wrote(hb, em.cnt["pe"])
                        a_g = EO("scalar", lambda e, f=f, hb=hb: e.activation(
                            ht_sb[:, f * 512:(f + 1) * 512], P[hb][:, 0:512], AF.Gelu),
                            waits=bk.read_wait(hb), inc="ac")
                        bk.read(hb, "ac", a_g)
                        ht_gelu[f] = a_g
                    for t in range(4):
                        ys = tcount % 3
                        tcount += 1
                        banks = (2 + 2 * ys, 3 + 2 * ys)
                        ti = tg * 4 + t
                        for hh in range(2):
                            bank = banks[hh]
                            w = bk.write_waits(bank)
                            for f in range(FQ):
                                EO("tensor", lambda e, f=f, t=t, hh=hh, b=b, bank=bank: e.matmul(
                                    P[bank][:, 0:512],
                                    ht_sb[:, f * 512 + t * 128: f * 512 + (t + 1) * 128],
                                    d_bf[:, b * 8192 + f * 1024 + hh * 512: b * 8192 + f * 1024 + (hh + 1) * 512],
                                    start=(f == 0), stop=(f == FQ - 1)),
                                    waits=(w if f == 0 else []) + [("ac", ht_gelu[f])],
                                    inc="pe" if f == FQ - 1 else None)
                            bk.wrote(bank, em.cnt["pe"])
                        pe_group_end[qi] = em.cnt["pe"]
                        for hh in range(2):
                            bank = banks[hh]
                            d_y = EO("vector", lambda e, ti=ti, hh=hh, e_i=e_i, bank=bank: e.scalar_tensor_tensor(
                                acc[:, ti * H + hh * 512: ti * H + (hh + 1) * 512],
                                P[bank][:, 0:512],
                                gates[:, ti * NE + e_i: ti * NE + e_i + 1],
                                acc[:, ti * H + hh * 512: ti * H + (hh + 1) * 512],
                                ALU.mult, ALU.add),
                                waits=bk.read_wait(bank) + [("dv", d_gates), ("dv", ao_evac[ti])], inc="dv")
                            bk.read(bank, "dv", d_y)
                if qi + 2 < NE * NQ:
                    wd_q[qi + 2] = emit_wdma(qi + 2, pe_group_end[qi])

        d_all = em.cnt["dv"]
        EO("sync", lambda e: e.dma_start(out=out.rearrange("(i p) c -> p i c", p=128), in_=acc3),
           waits=[("dv", d_all)], inc="st", amt=16)
        EO("sync", lambda e: e.wait_ge(em.sem["st"], em.cnt["st"]))
        em.play()
    return nc


def kernel(**inputs):
    f32 = np.float32
    x = np.ascontiguousarray(np.asarray(inputs["L0_attn_hidden_states"], f32).reshape(16384, H))
    wq = np.asarray(inputs["L0_attn_w_q"], f32)
    wk = np.asarray(inputs["L0_attn_w_k"], f32)
    wv = np.asarray(inputs["L0_attn_w_v"], f32)
    wo = np.asarray(inputs["L0_attn_w_o"], f32)
    base = {
        "gate_g": np.asarray(inputs["L0_attn_gate"], f32).reshape(1, 1),
        "wqT": np.ascontiguousarray(wq.T),
        "wkT": np.ascontiguousarray(wk.T),
        "wvT": np.ascontiguousarray(wv.T),
        "woT": np.ascontiguousarray(wo.T),
        "memw": np.ascontiguousarray(np.asarray(inputs["L0_attn_memory"], f32)),
        "memn": np.ascontiguousarray(np.asarray(inputs["L0_attn_memory_norm"], f32)),
        "su": np.ascontiguousarray(np.asarray(inputs["L0_ffn_shared_up"], f32).reshape(H, 4096)),
        "sd": np.ascontiguousarray(np.asarray(inputs["L0_ffn_shared_down"], f32).reshape(4096, H)),
        "ru": np.ascontiguousarray(np.asarray(inputs["L0_ffn_routed_up"], f32).reshape(8 * H, 4096)),
        "rd": np.ascontiguousarray(np.asarray(inputs["L0_ffn_routed_down"], f32).reshape(8 * 4096, H)),
        "rw": np.ascontiguousarray(np.asarray(inputs["L0_ffn_router_w"], f32)),
        "maskc": np.triu(np.ones((128, 128), f32)).astype(ml_dtypes.bfloat16),
        "identc": np.eye(128, dtype=f32),
        "onesrow": np.ones((1, 128), f32),
        "onecol": np.ones((128, 1), ml_dtypes.bfloat16),
    }
    in_maps = []
    for c in range(8):
        own = np.ascontiguousarray(x[c * TOK:(c + 1) * TOK])
        if c % 2 == 1:
            pred = np.ascontiguousarray(x[(c - 1) * TOK:c * TOK])
            cp = np.ones((1, 1), f32)
        else:
            pred = own
            cp = np.zeros((1, 1), f32)
        m = dict(base)
        m["x_own"] = own
        m["x_pred"] = pred
        m["cpred"] = cp
        in_maps.append(m)

    nc = build_program()
    res = run_bass_kernel_spmd(nc, in_maps, core_ids=list(range(8)))
    outs = [np.asarray(r["out"]) for r in res.results]
    return np.concatenate(outs, axis=0).reshape(4, 4096, H)


# version 08
# speedup vs baseline: 1.0484x; 1.0088x over previous
import numpy as np
import ml_dtypes
import concourse.bass as bass
import concourse.mybir as mybir
from concourse.bass_utils import run_bass_kernel_spmd
from contextlib import ExitStack

F32 = mybir.dt.float32
BF16 = mybir.dt.bfloat16
AF = mybir.ActivationFunctionType
ALU = mybir.AluOpType
AX = mybir.AxisListType

TOK = 2048
NT = 16
NTG = 4
NCH = 16
H = 1024
DM = 256
KC = 8
NE = 9
NQ = 4
FQ = 8
EPS = 1e-6


class Em:
    ENGS = ("tensor", "scalar", "vector", "gpsimd", "sync")

    def __init__(self, nc, ctx):
        self.nc = nc
        self.ops = {e: [] for e in self.ENGS}
        self.cnt = {}
        self.sem = {}
        self.ctx = ctx
        self.last_wait = {}

    def new_sem(self, name):
        self.sem[name] = self.ctx.enter_context(self.nc.semaphore(name))
        self.cnt[name] = 0

    SELF_SEM = {"scalar": "ac", "vector": "dv"}

    def op(self, eng, fn, waits=(), inc=None, amt=1):
        pruned = []
        ss = self.SELF_SEM.get(eng)
        if ss is not None and self.cnt.get(ss, 0) > 0:
            waits = list(waits) + [(ss, self.cnt[ss])]
        for s, v in waits:
            if v is None or v <= 0:
                continue
            k = (eng, s)
            if v > self.last_wait.get(k, 0):
                pruned.append((s, v))
                self.last_wait[k] = v
        if inc is not None:
            self.cnt[inc] += amt
        self.ops[eng].append((pruned, fn, inc, amt))
        return self.cnt[inc] if inc else None

    def play(self):
        block = self.ctx.enter_context(self.nc.Block())
        sems = self.sem

        def mk(eng_name):
            def body(eng):
                for pruned, fn, inc, amt in self.ops[eng_name]:
                    for s, v in pruned:
                        eng.wait_ge(sems[s], v)
                    ins = fn(eng)
                    if inc is not None:
                        ins.then_inc(sems[inc], amt)
            return body
        for e in self.ENGS:
            if self.ops[e]:
                getattr(block, e)(mk(e))


class Banks:

    def __init__(self):
        self.w = {i: 0 for i in range(8)}
        self.r = {i: {} for i in range(8)}

    def write_waits(self, i):
        return list(self.r[i].items())

    def wrote(self, i, pe_cnt):
        self.w[i] = pe_cnt
        self.r[i] = {}

    def read_wait(self, i):
        return [("pe", self.w[i])]

    def read(self, i, sem, cnt):
        self.r[i][sem] = cnt


def build_program():
    nc = bass.Bass("TRN2", target_bir_lowering=False)
    dt = nc.dram_tensor
    x_own = dt("x_own", [TOK, H], F32, kind="ExternalInput").ap()
    x_pred = dt("x_pred", [TOK, H], F32, kind="ExternalInput").ap()
    cpred = dt("cpred", [1, 1], F32, kind="ExternalInput").ap()
    gate_g = dt("gate_g", [1, 1], F32, kind="ExternalInput").ap()
    wqT = dt("wqT", [H, DM], F32, kind="ExternalInput").ap()
    wkT = dt("wkT", [H, DM], F32, kind="ExternalInput").ap()
    wvT = dt("wvT", [H, DM], F32, kind="ExternalInput").ap()
    woT = dt("woT", [DM, H], F32, kind="ExternalInput").ap()
    memw = dt("memw", [DM, DM], F32, kind="ExternalInput").ap()
    memn = dt("memn", [DM], F32, kind="ExternalInput").ap()
    su = dt("su", [H, 4096], F32, kind="ExternalInput").ap()
    sd = dt("sd", [4096, H], F32, kind="ExternalInput").ap()
    ru = dt("ru", [8 * H, 4096], F32, kind="ExternalInput").ap()
    rd = dt("rd", [8 * 4096, H], F32, kind="ExternalInput").ap()
    rw = dt("rw", [H, 8], F32, kind="ExternalInput").ap()
    maskc = dt("maskc", [128, 128], BF16, kind="ExternalInput").ap()
    identc = dt("identc", [128, 128], F32, kind="ExternalInput").ap()
    onesrow = dt("onesrow", [1, 128], F32, kind="ExternalInput").ap()
    onecol = dt("onecol", [128, 1], BF16, kind="ExternalInput").ap()
    out = dt("out", [TOK, H], F32, kind="ExternalOutput").ap()

    with ExitStack() as ctx:
        sb = lambda n, s, d: ctx.enter_context(nc.sbuf_tensor(n, s, d))
        acc = sb("acc", [128, NT * H], F32)
        xnfm = sb("xnfm", [128, KC * TOK], BF16)
        u_bf = sb("u_bf", [128, 2 * 8192], BF16)
        d_bf = sb("d_bf", [128, 2 * 8192], BF16)
        ht_sb = sb("ht_sb", [128, FQ * 512], BF16)
        emin = sb("emin", [128, 768], F32)
        eexp = sb("eexp", [128, 1024], F32)
        sq = sb("sq", [128, 1024], F32)
        sqf = sb("sqf", [128, 2 * 512], BF16)
        skf = sb("skf", [128, 2 * 512], BF16)
        ktm = sb("ktm", [128, 2 * 256], BF16)
        vtm = sb("vtm", [128, 2 * 256], BF16)
        s_sb = sb("s_sb", [128, 512], BF16)
        z_f = sb("z_f", [128, 2], F32)
        z_bf = sb("z_bf", [128, 2], BF16)
        comb_l = sb("comb_l", [128, 256], F32)
        cfm = sb("cfm", [128, 256], BF16)
        atm = sb("atm", [128, 128], BF16)
        rr = sb("rr", [128, 4], F32)
        eps_t = sb("eps_t", [128, 1], F32)
        stats = sb("stats", [128, NT * 8], F32)
        prs = sb("prs", [128, 2 * 16], F32)
        rstat = sb("rstat", [128, 2 * 8], F32)
        wqT_b = sb("wqT_b", [128, KC * 256], BF16)
        wkT_b = sb("wkT_b", [128, KC * 256], BF16)
        wvT_b = sb("wvT_b", [128, KC * 256], BF16)
        woT_b = sb("woT_b", [128, 2 * 1024], BF16)
        mem_b = sb("mem_b", [128, 2 * 256], BF16)
        memn_b = sb("memn_b", [128, 2], BF16)
        rw_b = sb("rw_b", [128, KC * 8], BF16)
        gate_s = sb("gate_s", [1, 1], F32)
        cp_s = sb("cp_s", [1, 1], F32)
        onec_b = sb("onec_b", [128, 1], BF16)
        mask_b = sb("mask_b", [128, 128], BF16)
        ident_f = sb("ident_f", [128, 128], F32)
        coef_r = sb("coef_r", [1, 4], F32)
        memn_r = sq[0:1, 0:256]
        ones_r = sq[0:1, 512:640]
        coef = sb("coef", [128, 4], F32)
        oscr = sb("oscr", [1, 8], F32)
        gates = sb("gates", [128, NT * NE], F32)
        P = [ctx.enter_context(nc.psum_tensor(f"P{i}", [128, 512], F32)) for i in range(8)]

        em = Em(nc, ctx)
        for s in ("ld", "wa", "wda", "wdb", "pe", "ac", "dv", "st"):
            em.new_sem(s)
        bk = Banks()
        EO = em.op
        pe_coef = [0]
        dv_coef = [0]

        acc3 = acc[:].rearrange("p (i c) -> p i c", i=NT)
        EO("sync", lambda e: e.dma_start(out=acc3, in_=x_pred.rearrange("(i p) c -> p i c", p=128)), inc="ld", amt=16)
        ld_predx = em.cnt["ld"]
        for dst, src in ((gate_s[:], gate_g), (cp_s[:], cpred), (ones_r, onesrow),
                         (ident_f[:], identc), (memn_r, memn.rearrange("(a c) -> a c", a=1)),
                         (mask_b[:], maskc), (onec_b[:], onecol)):
            EO("sync", lambda e, dst=dst, src=src: e.dma_start(out=dst, in_=src), inc="ld", amt=16)
        ld_consts = em.cnt["ld"]

        def cast3(dst_t, dst_groups, src_ap, p=128):
            dst = dst_t[:].rearrange("p (k c) -> p k c", k=dst_groups)
            src = src_ap.rearrange("(k p) c -> p k c", p=p)
            return lambda e: e.dma_start(out=dst, in_=src)
        EO("gpsimd", cast3(wqT_b, KC, wqT), inc="wa", amt=16)
        EO("gpsimd", cast3(wkT_b, KC, wkT), inc="wa", amt=16)
        EO("gpsimd", cast3(wvT_b, KC, wvT), inc="wa", amt=16)
        EO("gpsimd", cast3(woT_b, 2, woT), inc="wa", amt=16)
        EO("gpsimd", cast3(mem_b, 2, memw), inc="wa", amt=16)
        EO("gpsimd", cast3(rw_b, KC, rw), inc="wa", amt=16)
        EO("gpsimd", lambda e: e.dma_start(
            out=memn_b[:], in_=memn.rearrange("(k p) -> p k", p=128),
            allow_slow_non_contiguous=True), inc="wa", amt=16)
        wd_attn = ("wa", em.cnt["wa"])

        ru3 = ru.rearrange("(e k p) c -> e p k c", e=8, p=128)
        rd3 = rd.rearrange("(e f p) c -> e p f c", e=8, p=128)
        su3 = su.rearrange("(k p) c -> p k c", p=128)
        sd3 = sd.rearrange("(f p) c -> p f c", p=128)

        def emit_wdma(qi, pe_wait):
            e_, q = divmod(qi, NQ)
            b = qi % 2
            u_dst = u_bf[:, b * 8192:(b + 1) * 8192].rearrange("p (k c) -> p k c", k=KC)
            d_dst = d_bf[:, b * 8192:(b + 1) * 8192].rearrange("p (f c) -> p f c", f=FQ)
            if e_ == 0:
                u_src = su3[:, :, q * 1024:(q + 1) * 1024]
                d_src = sd3[:, q * FQ:(q + 1) * FQ, :]
            else:
                u_src = ru3[e_ - 1][:, :, q * 1024:(q + 1) * 1024]
                d_src = rd3[e_ - 1][:, q * FQ:(q + 1) * FQ, :]
            sname = "wda" if b == 0 else "wdb"
            EO("gpsimd", lambda e: e.dma_start(out=u_dst, in_=u_src),
               waits=[("pe", pe_wait)], inc=sname, amt=16)
            return (sname, EO("gpsimd", lambda e: e.dma_start(out=d_dst, in_=d_src), inc=sname, amt=16))

        wd_q = {}
        wd_q[0] = emit_wdma(0, 0)
        wd_q[1] = emit_wdma(1, 0)

        EO("vector", lambda e: e.memset(z_f[:], 0.0), inc="dv")
        EO("vector", lambda e: e.memset(eps_t[:], 1e-5), inc="dv")
        EO("vector", lambda e: e.memset(gates[:], 1.0), inc="dv")

        EO("scalar", lambda e: e.activation(oscr[:, 0:1], gate_s[:], AF.Exp, bias=0.0, scale=-1.0),
           waits=[("ld", ld_consts)], inc="ac")
        a_g = em.cnt["ac"]
        EO("vector", lambda e: e.tensor_scalar_add(oscr[:, 1:2], oscr[:, 0:1], 1.0),
           waits=[("ac", a_g)], inc="dv")
        EO("vector", lambda e: e.reciprocal(oscr[:, 2:3], oscr[:, 1:2]), inc="dv")
        EO("vector", lambda e: e.tensor_reduce(oscr[:, 3:4], memn_r, AX.X, ALU.add), inc="dv")
        EO("vector", lambda e: e.tensor_single_scalar(oscr[:, 4:5], oscr[:, 3:4], EPS, ALU.is_ge), inc="dv")
        EO("vector", lambda e: e.tensor_mul(coef_r[:, 1:2], oscr[:, 2:3], oscr[:, 4:5]), inc="dv")
        EO("vector", lambda e: e.tensor_scalar(coef_r[:, 2:3], oscr[:, 2:3], -1.0, 1.0, ALU.mult, ALU.add), inc="dv")
        EO("vector", lambda e: e.tensor_copy(coef_r[:, 0:1], cp_s[:]), inc="dv")
        d_coefr = em.cnt["dv"]
        n = EO("tensor", lambda e: e.matmul(P[4][:, 0:3], ones_r, coef_r[:, 0:3], start=True, stop=True),
               waits=[("dv", d_coefr), ("ld", ld_consts)] + bk.write_waits(4), inc="pe")
        bk.wrote(4, n)
        d_coef = EO("vector", lambda e: e.tensor_copy(coef[:, 0:3], P[4][:, 0:3]),
                    waits=bk.read_wait(4), inc="dv")
        bk.read(4, "dv", d_coef)
        pe_coef[0] = em.cnt["pe"]
        dv_coef[0] = d_coef

        xnfm4 = xnfm[:].rearrange("p (k t) -> p k t", k=KC)
        ln_prev_tp = [0]

        def emit_ln(i, own, extra_waits=()):
            so = i * 8
            src = acc[:, i * H:(i + 1) * H]
            w0 = list(extra_waits)
            EO("vector", lambda e: e.tensor_reduce(stats[:, so:so + 1], src, AX.X, ALU.add, negate=True),
               waits=w0, inc="dv")
            EO("scalar", lambda e: e.activation(sq[:], src, AF.Square),
               waits=w0 + [("pe", pe_coef[0]), ("dv", dv_coef[0])], inc="ac")
            a_sq = em.cnt["ac"]
            EO("vector", lambda e: e.tensor_reduce(stats[:, so + 2:so + 3], sq[:], AX.X, ALU.add),
               waits=[("ac", a_sq)], inc="dv")
            EO("vector", lambda e: e.tensor_scalar_mul(stats[:, so + 1:so + 2], stats[:, so:so + 1], 1.0 / H), inc="dv")
            EO("vector", lambda e: e.tensor_mul(stats[:, so + 3:so + 4], stats[:, so + 1:so + 2],
                                                stats[:, so + 1:so + 2]), inc="dv")
            EO("vector", lambda e: e.tensor_scalar(stats[:, so + 4:so + 5], stats[:, so + 2:so + 3],
                                                   1.0 / H, stats[:, so + 3:so + 4], ALU.mult, ALU.subtract), inc="dv")
            d_var = em.cnt["dv"]
            EO("scalar", lambda e: e.activation(stats[:, so + 5:so + 6], stats[:, so + 4:so + 5], AF.Ln, bias=eps_t[:]),
               waits=[("dv", d_var)], inc="ac")
            EO("scalar", lambda e: e.activation(stats[:, so + 6:so + 7], stats[:, so + 5:so + 6], AF.Exp,
                                                scale=-0.5), inc="ac")
            a_rs = em.cnt["ac"]
            EO("vector", lambda e: e.tensor_scalar(eexp[:], src, stats[:, so + 1:so + 2],
                                                   stats[:, so + 6:so + 7], ALU.add, ALU.mult),
               waits=[("ac", a_rs), ("pe", ln_prev_tp[0])], inc="dv")
            d_xn = em.cnt["dv"]
            if own:
                EO("vector", lambda e: e.tensor_add(src, src, eexp[:]), inc="dv")
            for half in range(2):
                for j in range(4):
                    k = half * 4 + j
                    w = [("dv", d_xn), ("ld", ld_consts)] + (bk.write_waits(7) if j == 0 else [])
                    n = EO("tensor", lambda e, k=k, j=j: e.transpose(
                        P[7][:, j * 128:(j + 1) * 128], eexp[:, k * 128:(k + 1) * 128], ident_f[:]),
                        waits=w, inc="pe" if j == 3 else None)
                bk.wrote(7, n)
                dst = xnfm4[:, half * 4:(half + 1) * 4, i * 128:(i + 1) * 128]
                srcp = P[7][:].rearrange("p (j c) -> p j c", j=4)
                d_ev = EO("vector", lambda e, dst=dst, srcp=srcp: e.tensor_copy(dst, srcp),
                          waits=bk.read_wait(7), inc="dv")
                bk.read(7, "dv", d_ev)
            ln_prev_tp[0] = n
            return d_ev

        for i in range(NT):
            emit_ln(i, own=False, extra_waits=[("ld", ld_consts)])
        d_predln = em.cnt["dv"]
        a_predln = em.cnt["ac"]
        pe_predln = em.cnt["pe"]

        def emit_kv(ci, pred, guard_ac, guard_pe):
            par = ci % 2
            bank = par
            er = 512
            w = bk.write_waits(bank) + [wd_attn, ("dv", d_coef)]
            for k in range(KC):
                EO("tensor", lambda e, k=k, par=par, ci=ci: e.matmul(
                    P[par][:, 0:256],
                    xnfm[:, k * TOK + ci * 128: k * TOK + (ci + 1) * 128],
                    wkT_b[:, k * 256:(k + 1) * 256],
                    start=(k == 0), stop=(k == KC - 1)),
                    waits=w if k == 0 else (), inc=None)
            for k in range(KC):
                n = EO("tensor", lambda e, k=k, par=par, ci=ci: e.matmul(
                    P[par][:, 256:512],
                    xnfm[:, k * TOK + ci * 128: k * TOK + (ci + 1) * 128],
                    wvT_b[:, k * 256:(k + 1) * 256],
                    start=(k == 0), stop=(k == KC - 1)), inc="pe" if k == KC - 1 else None)
            bk.wrote(bank, n)
            rw_ = bk.read_wait(bank)
            EO("vector", lambda e, par=par, er=er: e.tensor_scalar_min(
                emin[:, er:er + 256], P[par][:, 0:256], 0.0),
                waits=rw_ + [("ac", guard_ac)], inc="dv")
            d_min = em.cnt["dv"]
            EO("scalar", lambda e, er=er: e.activation(
                eexp[:, er:er + 256], emin[:, er:er + 256], AF.Exp),
                waits=[("dv", d_min), ("pe", guard_pe)], inc="ac")
            a_exp = em.cnt["ac"]
            if pred:
                EO("vector", lambda e, par=par, er=er: e.scalar_tensor_tensor(
                    eexp[:, 0:256], P[par][:, 0:256], 0.0, eexp[:, er:er + 256], ALU.max, ALU.add),
                    waits=[("ac", a_exp)], inc="dv")
                EO("vector", lambda e, par=par: e.tensor_scalar_mul(
                    ktm[:, par * 256:(par + 1) * 256], eexp[:, 0:256], coef[:, 0:1]), inc="dv")
                d_v = EO("vector", lambda e, par=par: e.tensor_scalar_mul(
                    vtm[:, par * 256:(par + 1) * 256], P[par][:, 256:512], coef[:, 0:1]), inc="dv")
            else:
                EO("vector", lambda e, par=par, er=er: e.scalar_tensor_tensor(
                    ktm[:, par * 256:(par + 1) * 256], P[par][:, 0:256], 0.0,
                    eexp[:, er:er + 256], ALU.max, ALU.add), waits=[("ac", a_exp)], inc="dv")
                d_v = EO("vector", lambda e, par=par: e.tensor_copy(
                    vtm[:, par * 256:(par + 1) * 256], P[par][:, 256:512]), inc="dv")
            bk.read(bank, "dv", d_v)
            return d_v, d_v

        s_started = [False]

        def emit_state(ci, d_k, a_v, a_snap, last):
            par = ci % 2
            w = [("dv", max(d_k, a_v))]
            if a_snap is not None:
                w.append(("ac", a_snap))
            w += bk.write_waits(6)
            s_mm = None
            for dc in range(2):
                s_mm = EO("tensor", lambda e, dc=dc, par=par, st=s_started[0]: e.matmul(
                    P[6][:, dc * 256:(dc + 1) * 256],
                    ktm[:, par * 256 + dc * 128: par * 256 + (dc + 1) * 128],
                    vtm[:, par * 256:(par + 1) * 256],
                    start=(not st), stop=False, skip_group_check=True),
                    waits=w if dc == 0 else (), inc="pe" if dc == 1 else None)
                s_started[0] = True
            bk.wrote(6, s_mm)
            w4 = bk.write_waits(4) + [("ld", 16 * 8)]
            n = None
            for dc in range(2):
                n = EO("tensor", lambda e, dc=dc, par=par: e.matmul(
                    P[4][:, 136 + dc:137 + dc],
                    ktm[:, par * 256 + dc * 128: par * 256 + (dc + 1) * 128],
                    onec_b[:], start=True, stop=True),
                    waits=w4 if dc == 0 else (), inc="pe" if dc == 1 else None)
            bk.wrote(4, n)
            d_z = EO("vector", lambda e: e.tensor_add(z_f[:], z_f[:], P[4][:, 136:138]),
                     waits=bk.read_wait(4), inc="dv")
            bk.read(4, "dv", d_z)
            return s_mm, d_z

        s_mm, d_z = None, None
        for ci in range(NCH):
            d_k, a_v = emit_kv(ci, pred=True, guard_ac=a_predln, guard_pe=pe_predln)
            s_mm, d_z = emit_state(ci, d_k, a_v, None, last=False)
        pe_predproj = em.cnt["pe"]
        dv_pred = em.cnt["dv"]
        ac_pred = em.cnt["ac"]
        a_snap = EO("scalar", lambda e: e.activation(s_sb[:], P[6][:], AF.Copy),
                    waits=bk.read_wait(6) + [("pe", s_mm)], inc="ac")
        bk.read(6, "ac", a_snap)
        EO("scalar", lambda e: e.activation(z_bf[:], z_f[:], AF.Copy), waits=[("dv", d_z)], inc="ac")
        a_zsnap = em.cnt["ac"]

        EO("sync", lambda e: e.dma_start(out=acc3, in_=x_own.rearrange("(i p) c -> p i c", p=128)),
           waits=[("dv", d_predln), ("ac", a_predln), ("pe", pe_predln)], inc="ld", amt=16)
        ld_own = em.cnt["ld"]
        xn_tile_dv = []
        for i in range(NT):
            xn_tile_dv.append(emit_ln(i, own=True,
                                      extra_waits=[("ld", ld_own), ("pe", pe_predproj),
                                                   ("dv", dv_pred), ("ac", ac_pred)]))
        d_ownln = em.cnt["dv"]
        a_ownln = em.cnt["ac"]
        pe_ownln = em.cnt["pe"]

        for i in range(NT):
            po = (i % 2) * 16
            ro = (i % 2) * 8
            w = bk.write_waits(5) + [("dv", xn_tile_dv[i]), wd_attn]
            n = None
            for k in range(KC):
                n = EO("tensor", lambda e, k=k, i=i: e.matmul(
                    P[5][:, 0:8],
                    xnfm[:, k * TOK + i * 128: k * TOK + (i + 1) * 128],
                    rw_b[:, k * 8:(k + 1) * 8],
                    start=(k == 0), stop=(k == KC - 1)),
                    waits=w if k == 0 else (), inc="pe" if k == KC - 1 else None)
            bk.wrote(5, n)
            rwt = bk.read_wait(5)
            EO("vector", lambda e, ro=ro: e.tensor_reduce(rstat[:, ro:ro + 1], P[5][:, 0:8], AX.X, ALU.max),
               waits=rwt, inc="dv")
            d_sub = EO("vector", lambda e, po=po, ro=ro: e.tensor_single_scalar(
                prs[:, po:po + 8], P[5][:, 0:8], rstat[:, ro:ro + 1], ALU.subtract), inc="dv")
            bk.read(5, "dv", d_sub)
            EO("scalar", lambda e, po=po: e.activation(prs[:, po + 8:po + 16], prs[:, po:po + 8], AF.Exp),
               waits=[("dv", d_sub)], inc="ac")
            a_e = em.cnt["ac"]
            EO("vector", lambda e, po=po, ro=ro: e.tensor_reduce(rstat[:, ro + 1:ro + 2], prs[:, po + 8:po + 16],
                                                                 AX.X, ALU.add), waits=[("ac", a_e)], inc="dv")
            EO("vector", lambda e, ro=ro: e.reciprocal(rstat[:, ro + 2:ro + 3], rstat[:, ro + 1:ro + 2]), inc="dv")
            EO("vector", lambda e, po=po, ro=ro: e.tensor_scalar_mul(prs[:, po:po + 8], prs[:, po + 8:po + 16],
                                                                     rstat[:, ro + 2:ro + 3]), inc="dv")
            EO("vector", lambda e, po=po, ro=ro: e.tensor_reduce(rstat[:, ro + 3:ro + 4], prs[:, po:po + 8],
                                                                 AX.X, ALU.max), inc="dv")
            EO("vector", lambda e, po=po, ro=ro: e.tensor_single_scalar(
                prs[:, po + 8:po + 16], prs[:, po:po + 8], rstat[:, ro + 3:ro + 4], ALU.is_equal), inc="dv")
            EO("vector", lambda e, po=po: e.scalar_tensor_tensor(
                prs[:, po + 8:po + 16], prs[:, po + 8:po + 16], -1e9, prs[:, po:po + 8],
                ALU.mult, ALU.add), inc="dv")
            EO("vector", lambda e, po=po, ro=ro: e.tensor_reduce(rstat[:, ro + 4:ro + 5], prs[:, po + 8:po + 16],
                                                                 AX.X, ALU.max), inc="dv")
            EO("vector", lambda e, i=i, po=po, ro=ro: e.scalar_tensor_tensor(
                gates[:, i * NE + 1:(i + 1) * NE], prs[:, po:po + 8], rstat[:, ro + 4:ro + 5], prs[:, po:po + 8],
                ALU.is_ge, ALU.mult), inc="dv")
        d_gates = em.cnt["dv"]

        def emit_fm_proj(tg, wT_b, dst):
            outs = []
            for dc in range(2):
                bank = 2 + dc
                w = bk.write_waits(bank) + [("dv", d_ownln), wd_attn]
                for k in range(KC):
                    EO("tensor", lambda e, k=k, dc=dc, tg=tg, bank=bank, wT_b=wT_b: e.matmul(
                        P[bank][:, 0:512],
                        wT_b[:, k * 256 + dc * 128: k * 256 + (dc + 1) * 128],
                        xnfm[:, k * TOK + tg * 512:k * TOK + (tg + 1) * 512],
                        start=(k == 0), stop=(k == KC - 1)),
                        waits=w if k == 0 else (), inc="pe" if k == KC - 1 else None)
                bk.wrote(bank, em.cnt["pe"])
                rwt = bk.read_wait(bank)
                EO("vector", lambda e, bank=bank: e.tensor_scalar_min(emin[:, 0:512], P[bank][:, 0:512], 0.0),
                   waits=rwt + [("ac", a_ownln)], inc="dv")
                d_min = em.cnt["dv"]
                EO("scalar", lambda e: e.activation(eexp[:, 0:512], emin[:, 0:512], AF.Exp),
                   waits=[("dv", d_min), ("pe", pe_ownln)], inc="ac")
                a_exp = em.cnt["ac"]
                d_s = EO("vector", lambda e, bank=bank, dc=dc, dst=dst: e.scalar_tensor_tensor(
                    dst[:, dc * 512:(dc + 1) * 512], P[bank][:, 0:512], 0.0,
                    eexp[:, 0:512], ALU.max, ALU.add), waits=[("ac", a_exp)], inc="dv")
                bk.read(bank, "dv", d_s)
                outs.append(d_s)
            return outs

        ao_evac = {}
        kv_ready = {}
        for tg in range(NTG):
            d_q = emit_fm_proj(tg, wqT_b, sqf)
            d_kf = emit_fm_proj(tg, wkT_b, skf)
            d_fm = max(d_q + d_kf)
            if tg == 0:
                kv_ready[0] = emit_kv(0, pred=False, guard_ac=a_ownln, guard_pe=pe_ownln)
            for cl in range(4):
                ci = tg * 4 + cl
                tl = cl * 128
                d_k, a_v = kv_ready[ci]
                w = bk.write_waits(4) + [("dv", d_fm)]
                n = None
                for dc in range(2):
                    n = EO("tensor", lambda e, dc=dc, tl=tl: e.matmul(
                        P[4][:, 0:128],
                        skf[:, dc * 512 + tl: dc * 512 + tl + 128],
                        sqf[:, dc * 512 + tl: dc * 512 + tl + 128],
                        start=(dc == 0), stop=(dc == 1)),
                        waits=w if dc == 0 else (), inc="pe" if dc == 1 else None)
                bk.wrote(4, n)
                d_atm = EO("vector", lambda e: e.tensor_tensor(atm[:], P[4][:, 0:128], mask_b[:], ALU.mult),
                           waits=bk.read_wait(4) + [("ld", ld_consts)], inc="dv")
                bk.read(4, "dv", d_atm)
                w = bk.write_waits(5) + [("dv", max(d_atm, a_v)), ("ac", a_snap)]
                EO("tensor", lambda e, ci=ci: e.matmul(
                    P[5][:, 0:256], atm[:], vtm[:, (ci % 2) * 256:((ci % 2) + 1) * 256],
                    start=True, stop=False), waits=w, inc=None)
                for dc in range(2):
                    EO("tensor", lambda e, dc=dc, tl=tl: e.matmul(
                        P[5][:, 0:256],
                        sqf[:, dc * 512 + tl: dc * 512 + tl + 128],
                        s_sb[:, dc * 256:(dc + 1) * 256],
                        start=False, stop=(dc == 1)), inc="pe" if dc == 1 else None)
                n = None
                for dc in range(2):
                    n = EO("tensor", lambda e, dc=dc, tl=tl: e.matmul(
                        P[5][:, 256:512],
                        sqf[:, dc * 512 + tl: dc * 512 + tl + 128],
                        mem_b[:, dc * 256:(dc + 1) * 256],
                        start=(dc == 0), stop=(dc == 1)), inc="pe" if dc == 1 else None)
                bk.wrote(5, n)
                w = bk.write_waits(4) + [("ac", a_zsnap), ("ld", ld_consts)]
                EO("tensor", lambda e: e.matmul(P[4][:, 128:129], atm[:], onec_b[:],
                                                start=True, stop=False), waits=w, inc=None)
                for dc in range(2):
                    EO("tensor", lambda e, dc=dc, tl=tl: e.matmul(
                        P[4][:, 128:129],
                        sqf[:, dc * 512 + tl: dc * 512 + tl + 128],
                        z_bf[:, dc:dc + 1], start=False, stop=(dc == 1)), inc="pe" if dc == 1 else None)
                n = None
                for dc in range(2):
                    n = EO("tensor", lambda e, dc=dc, tl=tl: e.matmul(
                        P[4][:, 132:133],
                        sqf[:, dc * 512 + tl: dc * 512 + tl + 128],
                        memn_b[:, dc:dc + 1], start=(dc == 0), stop=(dc == 1)), inc="pe" if dc == 1 else None)
                bk.wrote(4, n)
                if ci + 1 < NCH:
                    kv_ready[ci + 1] = emit_kv(ci + 1, pred=False, guard_ac=a_ownln, guard_pe=pe_ownln)
                rwt4 = bk.read_wait(4)
                rwt5 = bk.read_wait(5)
                EO("vector", lambda e: e.tensor_scalar_max(rr[:, 0:1], P[4][:, 128:129], EPS),
                   waits=rwt4, inc="dv")
                EO("vector", lambda e: e.tensor_scalar_max(rr[:, 1:2], P[4][:, 132:133], EPS), inc="dv")
                EO("vector", lambda e: e.reciprocal(rr[:, 2:3], rr[:, 0:1]), inc="dv")
                EO("vector", lambda e: e.reciprocal(rr[:, 3:4], rr[:, 1:2]), inc="dv")
                d_r = EO("vector", lambda e: e.tensor_mul(rr[:, 3:4], rr[:, 3:4], coef[:, 1:2]), inc="dv")
                bk.read(4, "dv", d_r)
                EO("vector", lambda e: e.tensor_scalar(comb_l[:], P[5][:, 0:256], rr[:, 2:3],
                                                       coef[:, 2:3], ALU.mult, ALU.mult),
                   waits=rwt5 + [("dv", d_r)], inc="dv")
                d_cf = EO("vector", lambda e: e.scalar_tensor_tensor(
                    comb_l[:], P[5][:, 256:512], rr[:, 3:4], comb_l[:], ALU.mult, ALU.add), inc="dv")
                d_cm = d_cf
                bk.read(5, "dv", d_cm)
                w = bk.write_waits(7) + [("dv", d_cf)]
                n = None
                for dc in range(2):
                    n = EO("tensor", lambda e, dc=dc: e.transpose(
                        P[7][:, dc * 128:(dc + 1) * 128], comb_l[:, dc * 128:(dc + 1) * 128], ident_f[:]),
                        waits=w if dc == 0 else (), inc="pe" if dc == 1 else None)
                bk.wrote(7, n)
                d_cfm = EO("vector", lambda e: e.tensor_copy(cfm[:], P[7][:, 0:256]),
                           waits=bk.read_wait(7), inc="dv")
                bk.read(7, "dv", d_cfm)
                d_ao = None
                for hh in range(2):
                    bank = 2 + hh
                    w = bk.write_waits(bank) + [("dv", d_cfm)]
                    for dc in range(2):
                        EO("tensor", lambda e, dc=dc, hh=hh, bank=bank: e.matmul(
                            P[bank][:, 0:512],
                            cfm[:, dc * 128:(dc + 1) * 128],
                            woT_b[:, dc * 1024 + hh * 512: dc * 1024 + (hh + 1) * 512],
                            start=(dc == 0), stop=(dc == 1)),
                            waits=w if dc == 0 else (), inc="pe" if dc == 1 else None)
                    bk.wrote(bank, em.cnt["pe"])
                    d_ao = EO("vector", lambda e, hh=hh, ci=ci, bank=bank: e.tensor_add(
                        acc[:, ci * H + hh * 512: ci * H + (hh + 1) * 512],
                        acc[:, ci * H + hh * 512: ci * H + (hh + 1) * 512],
                        P[bank][:, 0:512]), waits=bk.read_wait(bank), inc="dv")
                    bk.read(bank, "dv", d_ao)
                ao_evac[ci] = d_ao
                s_mm, d_z = emit_state(ci, d_k, a_v, a_snap, last=(ci == NCH - 1))
                if ci + 1 < NCH:
                    a_snap = EO("scalar", lambda e: e.activation(s_sb[:], P[6][:], AF.Copy),
                                waits=bk.read_wait(6) + [("pe", s_mm)], inc="ac")
                    bk.read(6, "ac", a_snap)
                    EO("scalar", lambda e: e.activation(z_bf[:], z_f[:], AF.Copy),
                       waits=[("dv", d_z)], inc="ac")
                    a_zsnap = em.cnt["ac"]

        ht_gelu = {}
        tcount = 0
        pe_group_end = {}
        for e_i in range(NE):
            for q in range(NQ):
                qi = e_i * NQ + q
                b = qi % 2
                for tg in range(NTG):
                    for f in range(FQ):
                        hb = f % 2
                        w = [wd_q[qi], ("dv", d_ownln)] + bk.write_waits(hb)
                        for k in range(KC):
                            EO("tensor", lambda e, k=k, f=f, b=b, tg=tg, hb=hb: e.matmul(
                                P[hb][:, 0:512],
                                u_bf[:, b * 8192 + k * 1024 + f * 128: b * 8192 + k * 1024 + (f + 1) * 128],
                                xnfm[:, k * TOK + tg * 512: k * TOK + (tg + 1) * 512],
                                start=(k == 0), stop=(k == KC - 1)),
                                waits=w if k == 0 else (), inc="pe" if k == KC - 1 else None)
                        bk.wrote(hb, em.cnt["pe"])
                        a_g = EO("scalar", lambda e, f=f, hb=hb: e.activation(
                            ht_sb[:, f * 512:(f + 1) * 512], P[hb][:, 0:512], AF.Gelu),
                            waits=bk.read_wait(hb), inc="ac")
                        bk.read(hb, "ac", a_g)
                        ht_gelu[f] = a_g
                    for t in range(4):
                        ys = tcount % 3
                        tcount += 1
                        banks = (2 + 2 * ys, 3 + 2 * ys)
                        ti = tg * 4 + t
                        for hh in range(2):
                            bank = banks[hh]
                            w = bk.write_waits(bank)
                            for f in range(FQ):
                                EO("tensor", lambda e, f=f, t=t, hh=hh, b=b, bank=bank: e.matmul(
                                    P[bank][:, 0:512],
                                    ht_sb[:, f * 512 + t * 128: f * 512 + (t + 1) * 128],
                                    d_bf[:, b * 8192 + f * 1024 + hh * 512: b * 8192 + f * 1024 + (hh + 1) * 512],
                                    start=(f == 0), stop=(f == FQ - 1)),
                                    waits=(w if f == 0 else []) + [("ac", ht_gelu[f])],
                                    inc="pe" if f == FQ - 1 else None)
                            bk.wrote(bank, em.cnt["pe"])
                        pe_group_end[qi] = em.cnt["pe"]
                        for hh in range(2):
                            bank = banks[hh]
                            d_y = EO("vector", lambda e, ti=ti, hh=hh, e_i=e_i, bank=bank: e.scalar_tensor_tensor(
                                acc[:, ti * H + hh * 512: ti * H + (hh + 1) * 512],
                                P[bank][:, 0:512],
                                gates[:, ti * NE + e_i: ti * NE + e_i + 1],
                                acc[:, ti * H + hh * 512: ti * H + (hh + 1) * 512],
                                ALU.mult, ALU.add),
                                waits=bk.read_wait(bank) + [("dv", d_gates), ("dv", ao_evac[ti])], inc="dv")
                            bk.read(bank, "dv", d_y)
                if qi + 2 < NE * NQ:
                    wd_q[qi + 2] = emit_wdma(qi + 2, pe_group_end[qi])

        d_all = em.cnt["dv"]
        EO("sync", lambda e: e.dma_start(out=out.rearrange("(i p) c -> p i c", p=128), in_=acc3),
           waits=[("dv", d_all)], inc="st", amt=16)
        EO("sync", lambda e: e.wait_ge(em.sem["st"], em.cnt["st"]))
        em.play()
    return nc


def kernel(**inputs):
    f32 = np.float32
    x = np.ascontiguousarray(np.asarray(inputs["L0_attn_hidden_states"], f32).reshape(16384, H))
    wq = np.asarray(inputs["L0_attn_w_q"], f32)
    wk = np.asarray(inputs["L0_attn_w_k"], f32)
    wv = np.asarray(inputs["L0_attn_w_v"], f32)
    wo = np.asarray(inputs["L0_attn_w_o"], f32)
    base = {
        "gate_g": np.asarray(inputs["L0_attn_gate"], f32).reshape(1, 1),
        "wqT": np.ascontiguousarray(wq.T),
        "wkT": np.ascontiguousarray(wk.T),
        "wvT": np.ascontiguousarray(wv.T),
        "woT": np.ascontiguousarray(wo.T),
        "memw": np.ascontiguousarray(np.asarray(inputs["L0_attn_memory"], f32)),
        "memn": np.ascontiguousarray(np.asarray(inputs["L0_attn_memory_norm"], f32)),
        "su": np.ascontiguousarray(np.asarray(inputs["L0_ffn_shared_up"], f32).reshape(H, 4096)),
        "sd": np.ascontiguousarray(np.asarray(inputs["L0_ffn_shared_down"], f32).reshape(4096, H)),
        "ru": np.ascontiguousarray(np.asarray(inputs["L0_ffn_routed_up"], f32).reshape(8 * H, 4096)),
        "rd": np.ascontiguousarray(np.asarray(inputs["L0_ffn_routed_down"], f32).reshape(8 * 4096, H)),
        "rw": np.ascontiguousarray(np.asarray(inputs["L0_ffn_router_w"], f32)),
        "maskc": np.triu(np.ones((128, 128), f32)).astype(ml_dtypes.bfloat16),
        "identc": np.eye(128, dtype=f32),
        "onesrow": np.ones((1, 128), f32),
        "onecol": np.ones((128, 1), ml_dtypes.bfloat16),
    }
    in_maps = []
    for c in range(8):
        own = np.ascontiguousarray(x[c * TOK:(c + 1) * TOK])
        if c % 2 == 1:
            pred = np.ascontiguousarray(x[(c - 1) * TOK:c * TOK])
            cp = np.ones((1, 1), f32)
        else:
            pred = own
            cp = np.zeros((1, 1), f32)
        m = dict(base)
        m["x_own"] = own
        m["x_pred"] = pred
        m["cpred"] = cp
        in_maps.append(m)

    nc = build_program()
    res = run_bass_kernel_spmd(nc, in_maps, core_ids=list(range(8)))
    outs = [np.asarray(r["out"]) for r in res.results]
    return np.concatenate(outs, axis=0).reshape(4, 4096, H)


# version 09
# speedup vs baseline: 1.0494x; 1.0010x over previous
import numpy as np
import ml_dtypes
import concourse.bass as bass
import concourse.mybir as mybir
from concourse.bass_utils import run_bass_kernel_spmd
from contextlib import ExitStack

F32 = mybir.dt.float32
BF16 = mybir.dt.bfloat16
AF = mybir.ActivationFunctionType
ALU = mybir.AluOpType
AX = mybir.AxisListType

TOK = 2048
NT = 16
NTG = 4
NCH = 16
H = 1024
DM = 256
KC = 8
NE = 9
NQ = 4
FQ = 8
EPS = 1e-6


class Em:
    ENGS = ("tensor", "scalar", "vector", "gpsimd", "sync")

    def __init__(self, nc, ctx):
        self.nc = nc
        self.ops = {e: [] for e in self.ENGS}
        self.cnt = {}
        self.sem = {}
        self.ctx = ctx
        self.last_wait = {}

    def new_sem(self, name):
        self.sem[name] = self.ctx.enter_context(self.nc.semaphore(name))
        self.cnt[name] = 0

    SELF_SEM = {"scalar": "ac", "vector": "dv"}

    def op(self, eng, fn, waits=(), inc=None, amt=1):
        pruned = []
        ss = self.SELF_SEM.get(eng)
        if ss is not None and self.cnt.get(ss, 0) > 0:
            waits = list(waits) + [(ss, self.cnt[ss])]
        for s, v in waits:
            if v is None or v <= 0:
                continue
            k = (eng, s)
            if v > self.last_wait.get(k, 0):
                pruned.append((s, v))
                self.last_wait[k] = v
        if inc is not None:
            self.cnt[inc] += amt
        self.ops[eng].append((pruned, fn, inc, amt))
        return self.cnt[inc] if inc else None

    def play(self):
        block = self.ctx.enter_context(self.nc.Block())
        sems = self.sem

        def mk(eng_name):
            def body(eng):
                for pruned, fn, inc, amt in self.ops[eng_name]:
                    for s, v in pruned:
                        eng.wait_ge(sems[s], v)
                    ins = fn(eng)
                    if inc is not None:
                        ins.then_inc(sems[inc], amt)
            return body
        for e in self.ENGS:
            if self.ops[e]:
                getattr(block, e)(mk(e))


class Banks:

    def __init__(self):
        self.w = {i: 0 for i in range(8)}
        self.r = {i: {} for i in range(8)}

    def write_waits(self, i):
        return list(self.r[i].items())

    def wrote(self, i, pe_cnt):
        self.w[i] = pe_cnt
        self.r[i] = {}

    def read_wait(self, i):
        return [("pe", self.w[i])]

    def read(self, i, sem, cnt):
        self.r[i][sem] = cnt


def build_program():
    nc = bass.Bass("TRN2", target_bir_lowering=False)
    dt = nc.dram_tensor
    x_own = dt("x_own", [TOK, H], F32, kind="ExternalInput").ap()
    x_pred = dt("x_pred", [TOK, H], F32, kind="ExternalInput").ap()
    cpred = dt("cpred", [1, 1], F32, kind="ExternalInput").ap()
    gate_g = dt("gate_g", [1, 1], F32, kind="ExternalInput").ap()
    wqT = dt("wqT", [H, DM], F32, kind="ExternalInput").ap()
    wkT = dt("wkT", [H, DM], F32, kind="ExternalInput").ap()
    wvT = dt("wvT", [H, DM], F32, kind="ExternalInput").ap()
    woT = dt("woT", [DM, H], F32, kind="ExternalInput").ap()
    memw = dt("memw", [DM, DM], F32, kind="ExternalInput").ap()
    memn = dt("memn", [DM], F32, kind="ExternalInput").ap()
    su = dt("su", [H, 4096], F32, kind="ExternalInput").ap()
    sd = dt("sd", [4096, H], F32, kind="ExternalInput").ap()
    ru = dt("ru", [8 * H, 4096], F32, kind="ExternalInput").ap()
    rd = dt("rd", [8 * 4096, H], F32, kind="ExternalInput").ap()
    rw = dt("rw", [H, 8], F32, kind="ExternalInput").ap()
    maskc = dt("maskc", [128, 128], BF16, kind="ExternalInput").ap()
    identc = dt("identc", [128, 128], F32, kind="ExternalInput").ap()
    onesrow = dt("onesrow", [1, 128], F32, kind="ExternalInput").ap()
    onecol = dt("onecol", [128, 1], BF16, kind="ExternalInput").ap()
    out = dt("out", [TOK, H], F32, kind="ExternalOutput").ap()

    with ExitStack() as ctx:
        sb = lambda n, s, d: ctx.enter_context(nc.sbuf_tensor(n, s, d))
        acc = sb("acc", [128, NT * H], F32)
        xnfm = sb("xnfm", [128, KC * TOK], BF16)
        u_bf = sb("u_bf", [128, 2 * 8192], BF16)
        d_bf = sb("d_bf", [128, 2 * 8192], BF16)
        ht_sb = sb("ht_sb", [128, FQ * 512], BF16)
        emin = sb("emin", [128, 768], F32)
        eexp = sb("eexp", [128, 1024], F32)
        sq = sb("sq", [128, 1024], F32)
        sqf = sb("sqf", [128, 2 * 512], BF16)
        skf = sb("skf", [128, 2 * 512], BF16)
        ktm = sb("ktm", [128, 2 * 256], BF16)
        vtm = sb("vtm", [128, 2 * 256], BF16)
        s_sb = sb("s_sb", [128, 512], BF16)
        z_f = sb("z_f", [128, 2], F32)
        z_bf = sb("z_bf", [128, 2], BF16)
        comb_l = sb("comb_l", [128, 256], F32)
        cfm = sb("cfm", [128, 256], BF16)
        atm = sb("atm", [128, 128], BF16)
        rr = sb("rr", [128, 4], F32)
        eps_t = sb("eps_t", [128, 1], F32)
        stats = sb("stats", [128, NT * 8], F32)
        prs = sb("prs", [128, 2 * 16], F32)
        rstat = sb("rstat", [128, 2 * 8], F32)
        wqT_b = sb("wqT_b", [128, KC * 256], BF16)
        wkT_b = sb("wkT_b", [128, KC * 256], BF16)
        wvT_b = sb("wvT_b", [128, KC * 256], BF16)
        woT_b = sb("woT_b", [128, 2 * 1024], BF16)
        mem_b = sb("mem_b", [128, 2 * 256], BF16)
        memn_b = sb("memn_b", [128, 2], BF16)
        rw_b = sb("rw_b", [128, KC * 8], BF16)
        gate_s = sb("gate_s", [1, 1], F32)
        cp_s = sb("cp_s", [1, 1], F32)
        onec_b = sb("onec_b", [128, 1], BF16)
        mask_b = sb("mask_b", [128, 128], BF16)
        ident_f = sb("ident_f", [128, 128], F32)
        coef_r = sb("coef_r", [1, 4], F32)
        memn_r = sq[0:1, 0:256]
        ones_r = sq[0:1, 512:640]
        coef = sb("coef", [128, 4], F32)
        oscr = sb("oscr", [1, 8], F32)
        gates = sb("gates", [128, NT * NE], F32)
        P = [ctx.enter_context(nc.psum_tensor(f"P{i}", [128, 512], F32)) for i in range(8)]

        em = Em(nc, ctx)
        for s in ("ld", "wa", "wda", "wdb", "pe", "ac", "dv", "st"):
            em.new_sem(s)
        bk = Banks()
        EO = em.op
        pe_coef = [0]
        dv_coef = [0]

        acc3 = acc[:].rearrange("p (i c) -> p i c", i=NT)
        EO("sync", lambda e: e.dma_start(out=acc3, in_=x_pred.rearrange("(i p) c -> p i c", p=128)), inc="ld", amt=16)
        ld_predx = em.cnt["ld"]
        for dst, src in ((gate_s[:], gate_g), (cp_s[:], cpred), (ones_r, onesrow),
                         (ident_f[:], identc), (memn_r, memn.rearrange("(a c) -> a c", a=1)),
                         (mask_b[:], maskc), (onec_b[:], onecol)):
            EO("sync", lambda e, dst=dst, src=src: e.dma_start(out=dst, in_=src), inc="ld", amt=16)
        ld_consts = em.cnt["ld"]

        def cast3(dst_t, dst_groups, src_ap, p=128):
            dst = dst_t[:].rearrange("p (k c) -> p k c", k=dst_groups)
            src = src_ap.rearrange("(k p) c -> p k c", p=p)
            return lambda e: e.dma_start(out=dst, in_=src)
        EO("gpsimd", cast3(wqT_b, KC, wqT), inc="wa", amt=16)
        EO("gpsimd", cast3(wkT_b, KC, wkT), inc="wa", amt=16)
        EO("gpsimd", cast3(wvT_b, KC, wvT), inc="wa", amt=16)
        EO("gpsimd", cast3(woT_b, 2, woT), inc="wa", amt=16)
        EO("gpsimd", cast3(mem_b, 2, memw), inc="wa", amt=16)
        EO("gpsimd", cast3(rw_b, KC, rw), inc="wa", amt=16)
        EO("gpsimd", lambda e: e.dma_start(
            out=memn_b[:], in_=memn.rearrange("(k p) -> p k", p=128),
            allow_slow_non_contiguous=True), inc="wa", amt=16)
        wd_attn = ("wa", em.cnt["wa"])

        ru3 = ru.rearrange("(e k p) c -> e p k c", e=8, p=128)
        rd3 = rd.rearrange("(e f p) c -> e p f c", e=8, p=128)
        su3 = su.rearrange("(k p) c -> p k c", p=128)
        sd3 = sd.rearrange("(f p) c -> p f c", p=128)

        def emit_wdma(qi, pe_wait):
            e_, q = divmod(qi, NQ)
            b = qi % 2
            u_dst = u_bf[:, b * 8192:(b + 1) * 8192].rearrange("p (k c) -> p k c", k=KC)
            d_dst = d_bf[:, b * 8192:(b + 1) * 8192].rearrange("p (f c) -> p f c", f=FQ)
            if e_ == 0:
                u_src = su3[:, :, q * 1024:(q + 1) * 1024]
                d_src = sd3[:, q * FQ:(q + 1) * FQ, :]
            else:
                u_src = ru3[e_ - 1][:, :, q * 1024:(q + 1) * 1024]
                d_src = rd3[e_ - 1][:, q * FQ:(q + 1) * FQ, :]
            sname = "wda" if b == 0 else "wdb"
            EO("gpsimd", lambda e: e.dma_start(out=u_dst, in_=u_src),
               waits=[("pe", pe_wait)], inc=sname, amt=16)
            return (sname, EO("gpsimd", lambda e: e.dma_start(out=d_dst, in_=d_src), inc=sname, amt=16))

        wd_q = {}
        wd_q[0] = emit_wdma(0, 0)
        wd_q[1] = emit_wdma(1, 0)

        EO("vector", lambda e: e.memset(z_f[:], 0.0), inc="dv")
        EO("vector", lambda e: e.memset(eps_t[:], 1e-5), inc="dv")
        EO("vector", lambda e: e.memset(gates[:], 1.0), inc="dv")

        EO("scalar", lambda e: e.activation(oscr[:, 0:1], gate_s[:], AF.Exp, bias=0.0, scale=-1.0),
           waits=[("ld", ld_consts)], inc="ac")
        a_g = em.cnt["ac"]
        EO("vector", lambda e: e.tensor_scalar_add(oscr[:, 1:2], oscr[:, 0:1], 1.0),
           waits=[("ac", a_g)], inc="dv")
        EO("vector", lambda e: e.reciprocal(oscr[:, 2:3], oscr[:, 1:2]), inc="dv")
        EO("vector", lambda e: e.tensor_reduce(oscr[:, 3:4], memn_r, AX.X, ALU.add), inc="dv")
        EO("vector", lambda e: e.tensor_single_scalar(oscr[:, 4:5], oscr[:, 3:4], EPS, ALU.is_ge), inc="dv")
        EO("vector", lambda e: e.tensor_mul(coef_r[:, 1:2], oscr[:, 2:3], oscr[:, 4:5]), inc="dv")
        EO("vector", lambda e: e.tensor_scalar(coef_r[:, 2:3], oscr[:, 2:3], -1.0, 1.0, ALU.mult, ALU.add), inc="dv")
        EO("vector", lambda e: e.tensor_copy(coef_r[:, 0:1], cp_s[:]), inc="dv")
        d_coefr = em.cnt["dv"]
        n = EO("tensor", lambda e: e.matmul(P[4][:, 0:3], ones_r, coef_r[:, 0:3], start=True, stop=True),
               waits=[("dv", d_coefr), ("ld", ld_consts)] + bk.write_waits(4), inc="pe")
        bk.wrote(4, n)
        d_coef = EO("vector", lambda e: e.tensor_copy(coef[:, 0:3], P[4][:, 0:3]),
                    waits=bk.read_wait(4), inc="dv")
        bk.read(4, "dv", d_coef)
        pe_coef[0] = em.cnt["pe"]
        dv_coef[0] = d_coef

        xnfm4 = xnfm[:].rearrange("p (k t) -> p k t", k=KC)
        ln_prev_tp = [0]

        def emit_ln_s1(i, own, extra_waits=()):
            so = i * 8
            src = acc[:, i * H:(i + 1) * H]
            w0 = list(extra_waits)
            EO("vector", lambda e: e.tensor_reduce(stats[:, so:so + 1], src, AX.X, ALU.add, negate=True),
               waits=w0, inc="dv")
            sqr = sq[:, (i % 2) * 512:(i % 2) * 512 + 512]
            pb = 5 if i % 2 == 0 else 2
            EO("scalar", lambda e, sqr=sqr: e.activation(sqr, src[:, 0:512], AF.Square),
               waits=w0 + [("pe", pe_coef[0]), ("dv", dv_coef[0])], inc="ac")
            EO("scalar", lambda e, pb=pb: e.activation(P[pb][:, 0:512], src[:, 512:1024], AF.Square),
               waits=bk.write_waits(pb), inc="ac")
            a_sq = em.cnt["ac"]
            bk.r[pb] = {}
            EO("vector", lambda e, sqr=sqr: e.tensor_reduce(stats[:, so + 2:so + 3], sqr, AX.X, ALU.add),
               waits=[("ac", a_sq)], inc="dv")
            d_r2 = EO("vector", lambda e, pb=pb: e.tensor_reduce(stats[:, so + 3:so + 4], P[pb][:, 0:512],
                                                                 AX.X, ALU.add), waits=[("ac", a_sq)], inc="dv")
            bk.read(pb, "dv", d_r2)
            EO("vector", lambda e: e.tensor_add(stats[:, so + 2:so + 3], stats[:, so + 2:so + 3],
                                                stats[:, so + 3:so + 4]), inc="dv")
            EO("vector", lambda e: e.tensor_scalar_mul(stats[:, so + 1:so + 2], stats[:, so:so + 1], 1.0 / H), inc="dv")
            EO("vector", lambda e: e.tensor_mul(stats[:, so + 3:so + 4], stats[:, so + 1:so + 2],
                                                stats[:, so + 1:so + 2]), inc="dv")
            EO("vector", lambda e: e.tensor_scalar(stats[:, so + 4:so + 5], stats[:, so + 2:so + 3],
                                                   1.0 / H, stats[:, so + 3:so + 4], ALU.mult, ALU.subtract), inc="dv")
            return em.cnt["dv"]

        def emit_ln_s2(i, own, d_var):
            so = i * 8
            src = acc[:, i * H:(i + 1) * H]
            EO("scalar", lambda e: e.activation(stats[:, so + 5:so + 6], stats[:, so + 4:so + 5], AF.Ln, bias=eps_t[:]),
               waits=[("dv", d_var)], inc="ac")
            EO("scalar", lambda e: e.activation(stats[:, so + 6:so + 7], stats[:, so + 5:so + 6], AF.Exp,
                                                scale=-0.5), inc="ac")
            a_rs = em.cnt["ac"]
            EO("vector", lambda e: e.tensor_scalar(eexp[:], src, stats[:, so + 1:so + 2],
                                                   stats[:, so + 6:so + 7], ALU.add, ALU.mult),
               waits=[("ac", a_rs), ("pe", ln_prev_tp[0])], inc="dv")
            d_xn = em.cnt["dv"]
            if own:
                EO("vector", lambda e: e.tensor_add(src, src, eexp[:]), inc="dv")
            for half in range(2):
                for j in range(4):
                    k = half * 4 + j
                    w = [("dv", d_xn), ("ld", ld_consts)] + (bk.write_waits(7) if j == 0 else [])
                    n = EO("tensor", lambda e, k=k, j=j: e.transpose(
                        P[7][:, j * 128:(j + 1) * 128], eexp[:, k * 128:(k + 1) * 128], ident_f[:]),
                        waits=w, inc="pe" if j == 3 else None)
                bk.wrote(7, n)
                dst = xnfm4[:, half * 4:(half + 1) * 4, i * 128:(i + 1) * 128]
                srcp = P[7][:].rearrange("p (j c) -> p j c", j=4)
                d_ev = EO("vector", lambda e, dst=dst, srcp=srcp: e.tensor_copy(dst, srcp),
                          waits=bk.read_wait(7), inc="dv")
                bk.read(7, "dv", d_ev)
            ln_prev_tp[0] = n
            return d_ev

        def emit_ln_phase(own, extra_waits=()):
            dvars = {}
            outs = []
            dvars[0] = emit_ln_s1(0, own, extra_waits)
            for i in range(NT):
                if i + 1 < NT:
                    dvars[i + 1] = emit_ln_s1(i + 1, own, extra_waits)
                outs.append(emit_ln_s2(i, own, dvars[i]))
            return outs

        emit_ln_phase(own=False, extra_waits=[("ld", ld_consts)])
        d_predln = em.cnt["dv"]
        a_predln = em.cnt["ac"]
        pe_predln = em.cnt["pe"]

        def emit_kv(ci, pred, guard_ac, guard_pe):
            par = ci % 2
            bank = par
            er = 512
            w = bk.write_waits(bank) + [wd_attn, ("dv", d_coef)]
            for k in range(KC):
                EO("tensor", lambda e, k=k, par=par, ci=ci: e.matmul(
                    P[par][:, 0:256],
                    xnfm[:, k * TOK + ci * 128: k * TOK + (ci + 1) * 128],
                    wkT_b[:, k * 256:(k + 1) * 256],
                    start=(k == 0), stop=(k == KC - 1)),
                    waits=w if k == 0 else (), inc=None)
            for k in range(KC):
                n = EO("tensor", lambda e, k=k, par=par, ci=ci: e.matmul(
                    P[par][:, 256:512],
                    xnfm[:, k * TOK + ci * 128: k * TOK + (ci + 1) * 128],
                    wvT_b[:, k * 256:(k + 1) * 256],
                    start=(k == 0), stop=(k == KC - 1)), inc="pe" if k == KC - 1 else None)
            bk.wrote(bank, n)
            rw_ = bk.read_wait(bank)
            EO("vector", lambda e, par=par, er=er: e.tensor_scalar_min(
                emin[:, er:er + 256], P[par][:, 0:256], 0.0),
                waits=rw_ + [("ac", guard_ac)], inc="dv")
            d_min = em.cnt["dv"]
            EO("scalar", lambda e, er=er: e.activation(
                eexp[:, er:er + 256], emin[:, er:er + 256], AF.Exp),
                waits=[("dv", d_min), ("pe", guard_pe)], inc="ac")
            a_exp = em.cnt["ac"]
            if pred:
                EO("vector", lambda e, par=par, er=er: e.scalar_tensor_tensor(
                    eexp[:, 0:256], P[par][:, 0:256], 0.0, eexp[:, er:er + 256], ALU.max, ALU.add),
                    waits=[("ac", a_exp)], inc="dv")
                EO("vector", lambda e, par=par: e.tensor_scalar_mul(
                    ktm[:, par * 256:(par + 1) * 256], eexp[:, 0:256], coef[:, 0:1]), inc="dv")
                d_v = EO("vector", lambda e, par=par: e.tensor_scalar_mul(
                    vtm[:, par * 256:(par + 1) * 256], P[par][:, 256:512], coef[:, 0:1]), inc="dv")
            else:
                EO("vector", lambda e, par=par, er=er: e.scalar_tensor_tensor(
                    ktm[:, par * 256:(par + 1) * 256], P[par][:, 0:256], 0.0,
                    eexp[:, er:er + 256], ALU.max, ALU.add), waits=[("ac", a_exp)], inc="dv")
                d_v = EO("vector", lambda e, par=par: e.tensor_copy(
                    vtm[:, par * 256:(par + 1) * 256], P[par][:, 256:512]), inc="dv")
            bk.read(bank, "dv", d_v)
            return d_v, d_v

        s_started = [False]

        def emit_state(ci, d_k, a_v, a_snap, last):
            par = ci % 2
            w = [("dv", max(d_k, a_v))]
            if a_snap is not None:
                w.append(("ac", a_snap))
            w += bk.write_waits(6)
            s_mm = None
            for dc in range(2):
                s_mm = EO("tensor", lambda e, dc=dc, par=par, st=s_started[0]: e.matmul(
                    P[6][:, dc * 256:(dc + 1) * 256],
                    ktm[:, par * 256 + dc * 128: par * 256 + (dc + 1) * 128],
                    vtm[:, par * 256:(par + 1) * 256],
                    start=(not st), stop=False, skip_group_check=True),
                    waits=w if dc == 0 else (), inc="pe" if dc == 1 else None)
                s_started[0] = True
            bk.wrote(6, s_mm)
            w4 = bk.write_waits(4) + [("ld", 16 * 8)]
            n = None
            for dc in range(2):
                n = EO("tensor", lambda e, dc=dc, par=par: e.matmul(
                    P[4][:, 136 + dc:137 + dc],
                    ktm[:, par * 256 + dc * 128: par * 256 + (dc + 1) * 128],
                    onec_b[:], start=True, stop=True),
                    waits=w4 if dc == 0 else (), inc="pe" if dc == 1 else None)
            bk.wrote(4, n)
            d_z = EO("vector", lambda e: e.tensor_add(z_f[:], z_f[:], P[4][:, 136:138]),
                     waits=bk.read_wait(4), inc="dv")
            bk.read(4, "dv", d_z)
            return s_mm, d_z

        s_mm, d_z = None, None
        for ci in range(NCH):
            d_k, a_v = emit_kv(ci, pred=True, guard_ac=a_predln, guard_pe=pe_predln)
            s_mm, d_z = emit_state(ci, d_k, a_v, None, last=False)
        pe_predproj = em.cnt["pe"]
        dv_pred = em.cnt["dv"]
        ac_pred = em.cnt["ac"]
        a_snap = EO("scalar", lambda e: e.activation(s_sb[:], P[6][:], AF.Copy),
                    waits=bk.read_wait(6) + [("pe", s_mm)], inc="ac")
        bk.read(6, "ac", a_snap)
        EO("scalar", lambda e: e.activation(z_bf[:], z_f[:], AF.Copy), waits=[("dv", d_z)], inc="ac")
        a_zsnap = em.cnt["ac"]

        EO("sync", lambda e: e.dma_start(out=acc3, in_=x_own.rearrange("(i p) c -> p i c", p=128)),
           waits=[("dv", d_predln), ("ac", a_predln), ("pe", pe_predln)], inc="ld", amt=16)
        ld_own = em.cnt["ld"]
        xn_tile_dv = emit_ln_phase(own=True,
                                   extra_waits=[("ld", ld_own), ("pe", pe_predproj),
                                                ("dv", dv_pred), ("ac", ac_pred)])
        d_ownln = em.cnt["dv"]
        a_ownln = em.cnt["ac"]
        pe_ownln = em.cnt["pe"]

        for i in range(NT):
            po = (i % 2) * 16
            ro = (i % 2) * 8
            w = bk.write_waits(5) + [("dv", xn_tile_dv[i]), wd_attn]
            n = None
            for k in range(KC):
                n = EO("tensor", lambda e, k=k, i=i: e.matmul(
                    P[5][:, 0:8],
                    xnfm[:, k * TOK + i * 128: k * TOK + (i + 1) * 128],
                    rw_b[:, k * 8:(k + 1) * 8],
                    start=(k == 0), stop=(k == KC - 1)),
                    waits=w if k == 0 else (), inc="pe" if k == KC - 1 else None)
            bk.wrote(5, n)
            rwt = bk.read_wait(5)
            EO("vector", lambda e, ro=ro: e.tensor_reduce(rstat[:, ro:ro + 1], P[5][:, 0:8], AX.X, ALU.max),
               waits=rwt, inc="dv")
            d_sub = EO("vector", lambda e, po=po, ro=ro: e.tensor_single_scalar(
                prs[:, po:po + 8], P[5][:, 0:8], rstat[:, ro:ro + 1], ALU.subtract), inc="dv")
            bk.read(5, "dv", d_sub)
            EO("scalar", lambda e, po=po: e.activation(prs[:, po + 8:po + 16], prs[:, po:po + 8], AF.Exp),
               waits=[("dv", d_sub)], inc="ac")
            a_e = em.cnt["ac"]
            EO("vector", lambda e, po=po, ro=ro: e.tensor_reduce(rstat[:, ro + 1:ro + 2], prs[:, po + 8:po + 16],
                                                                 AX.X, ALU.add), waits=[("ac", a_e)], inc="dv")
            EO("vector", lambda e, ro=ro: e.reciprocal(rstat[:, ro + 2:ro + 3], rstat[:, ro + 1:ro + 2]), inc="dv")
            EO("vector", lambda e, po=po, ro=ro: e.tensor_scalar_mul(prs[:, po:po + 8], prs[:, po + 8:po + 16],
                                                                     rstat[:, ro + 2:ro + 3]), inc="dv")
            EO("vector", lambda e, po=po, ro=ro: e.tensor_reduce(rstat[:, ro + 3:ro + 4], prs[:, po:po + 8],
                                                                 AX.X, ALU.max), inc="dv")
            EO("vector", lambda e, po=po, ro=ro: e.tensor_single_scalar(
                prs[:, po + 8:po + 16], prs[:, po:po + 8], rstat[:, ro + 3:ro + 4], ALU.is_equal), inc="dv")
            EO("vector", lambda e, po=po: e.scalar_tensor_tensor(
                prs[:, po + 8:po + 16], prs[:, po + 8:po + 16], -1e9, prs[:, po:po + 8],
                ALU.mult, ALU.add), inc="dv")
            EO("vector", lambda e, po=po, ro=ro: e.tensor_reduce(rstat[:, ro + 4:ro + 5], prs[:, po + 8:po + 16],
                                                                 AX.X, ALU.max), inc="dv")
            EO("vector", lambda e, i=i, po=po, ro=ro: e.scalar_tensor_tensor(
                gates[:, i * NE + 1:(i + 1) * NE], prs[:, po:po + 8], rstat[:, ro + 4:ro + 5], prs[:, po:po + 8],
                ALU.is_ge, ALU.mult), inc="dv")
        d_gates = em.cnt["dv"]

        def emit_fm_proj(tg, wT_b, dst):
            outs = []
            for dc in range(2):
                bank = 2 + dc
                w = bk.write_waits(bank) + [("dv", d_ownln), wd_attn]
                for k in range(KC):
                    EO("tensor", lambda e, k=k, dc=dc, tg=tg, bank=bank, wT_b=wT_b: e.matmul(
                        P[bank][:, 0:512],
                        wT_b[:, k * 256 + dc * 128: k * 256 + (dc + 1) * 128],
                        xnfm[:, k * TOK + tg * 512:k * TOK + (tg + 1) * 512],
                        start=(k == 0), stop=(k == KC - 1)),
                        waits=w if k == 0 else (), inc="pe" if k == KC - 1 else None)
                bk.wrote(bank, em.cnt["pe"])
                rwt = bk.read_wait(bank)
                EO("vector", lambda e, bank=bank: e.tensor_scalar_min(emin[:, 0:512], P[bank][:, 0:512], 0.0),
                   waits=rwt + [("ac", a_ownln)], inc="dv")
                d_min = em.cnt["dv"]
                EO("scalar", lambda e: e.activation(eexp[:, 0:512], emin[:, 0:512], AF.Exp),
                   waits=[("dv", d_min), ("pe", pe_ownln)], inc="ac")
                a_exp = em.cnt["ac"]
                d_s = EO("vector", lambda e, bank=bank, dc=dc, dst=dst: e.scalar_tensor_tensor(
                    dst[:, dc * 512:(dc + 1) * 512], P[bank][:, 0:512], 0.0,
                    eexp[:, 0:512], ALU.max, ALU.add), waits=[("ac", a_exp)], inc="dv")
                bk.read(bank, "dv", d_s)
                outs.append(d_s)
            return outs

        ao_evac = {}
        kv_ready = {}
        for tg in range(NTG):
            d_q = emit_fm_proj(tg, wqT_b, sqf)
            d_kf = emit_fm_proj(tg, wkT_b, skf)
            d_fm = max(d_q + d_kf)
            if tg == 0:
                kv_ready[0] = emit_kv(0, pred=False, guard_ac=a_ownln, guard_pe=pe_ownln)
            for cl in range(4):
                ci = tg * 4 + cl
                tl = cl * 128
                d_k, a_v = kv_ready[ci]
                w = bk.write_waits(4) + [("dv", d_fm)]
                n = None
                for dc in range(2):
                    n = EO("tensor", lambda e, dc=dc, tl=tl: e.matmul(
                        P[4][:, 0:128],
                        skf[:, dc * 512 + tl: dc * 512 + tl + 128],
                        sqf[:, dc * 512 + tl: dc * 512 + tl + 128],
                        start=(dc == 0), stop=(dc == 1)),
                        waits=w if dc == 0 else (), inc="pe" if dc == 1 else None)
                bk.wrote(4, n)
                d_atm = EO("vector", lambda e: e.tensor_tensor(atm[:], P[4][:, 0:128], mask_b[:], ALU.mult),
                           waits=bk.read_wait(4) + [("ld", ld_consts)], inc="dv")
                bk.read(4, "dv", d_atm)
                w = bk.write_waits(5) + [("dv", max(d_atm, a_v)), ("ac", a_snap)]
                EO("tensor", lambda e, ci=ci: e.matmul(
                    P[5][:, 0:256], atm[:], vtm[:, (ci % 2) * 256:((ci % 2) + 1) * 256],
                    start=True, stop=False), waits=w, inc=None)
                for dc in range(2):
                    EO("tensor", lambda e, dc=dc, tl=tl: e.matmul(
                        P[5][:, 0:256],
                        sqf[:, dc * 512 + tl: dc * 512 + tl + 128],
                        s_sb[:, dc * 256:(dc + 1) * 256],
                        start=False, stop=(dc == 1)), inc="pe" if dc == 1 else None)
                n = None
                for dc in range(2):
                    n = EO("tensor", lambda e, dc=dc, tl=tl: e.matmul(
                        P[5][:, 256:512],
                        sqf[:, dc * 512 + tl: dc * 512 + tl + 128],
                        mem_b[:, dc * 256:(dc + 1) * 256],
                        start=(dc == 0), stop=(dc == 1)), inc="pe" if dc == 1 else None)
                bk.wrote(5, n)
                w = bk.write_waits(4) + [("ac", a_zsnap), ("ld", ld_consts)]
                EO("tensor", lambda e: e.matmul(P[4][:, 128:129], atm[:], onec_b[:],
                                                start=True, stop=False), waits=w, inc=None)
                for dc in range(2):
                    EO("tensor", lambda e, dc=dc, tl=tl: e.matmul(
                        P[4][:, 128:129],
                        sqf[:, dc * 512 + tl: dc * 512 + tl + 128],
                        z_bf[:, dc:dc + 1], start=False, stop=(dc == 1)), inc="pe" if dc == 1 else None)
                n = None
                for dc in range(2):
                    n = EO("tensor", lambda e, dc=dc, tl=tl: e.matmul(
                        P[4][:, 132:133],
                        sqf[:, dc * 512 + tl: dc * 512 + tl + 128],
                        memn_b[:, dc:dc + 1], start=(dc == 0), stop=(dc == 1)), inc="pe" if dc == 1 else None)
                bk.wrote(4, n)
                if ci + 1 < NCH:
                    kv_ready[ci + 1] = emit_kv(ci + 1, pred=False, guard_ac=a_ownln, guard_pe=pe_ownln)
                rwt4 = bk.read_wait(4)
                rwt5 = bk.read_wait(5)
                EO("vector", lambda e: e.tensor_scalar_max(rr[:, 0:1], P[4][:, 128:129], EPS),
                   waits=rwt4, inc="dv")
                EO("vector", lambda e: e.tensor_scalar_max(rr[:, 1:2], P[4][:, 132:133], EPS), inc="dv")
                EO("vector", lambda e: e.reciprocal(rr[:, 2:3], rr[:, 0:1]), inc="dv")
                EO("vector", lambda e: e.reciprocal(rr[:, 3:4], rr[:, 1:2]), inc="dv")
                d_r = EO("vector", lambda e: e.tensor_mul(rr[:, 3:4], rr[:, 3:4], coef[:, 1:2]), inc="dv")
                bk.read(4, "dv", d_r)
                EO("vector", lambda e: e.tensor_scalar(comb_l[:], P[5][:, 0:256], rr[:, 2:3],
                                                       coef[:, 2:3], ALU.mult, ALU.mult),
                   waits=rwt5 + [("dv", d_r)], inc="dv")
                d_cf = EO("vector", lambda e: e.scalar_tensor_tensor(
                    comb_l[:], P[5][:, 256:512], rr[:, 3:4], comb_l[:], ALU.mult, ALU.add), inc="dv")
                d_cm = d_cf
                bk.read(5, "dv", d_cm)
                w = bk.write_waits(7) + [("dv", d_cf)]
                n = None
                for dc in range(2):
                    n = EO("tensor", lambda e, dc=dc: e.transpose(
                        P[7][:, dc * 128:(dc + 1) * 128], comb_l[:, dc * 128:(dc + 1) * 128], ident_f[:]),
                        waits=w if dc == 0 else (), inc="pe" if dc == 1 else None)
                bk.wrote(7, n)
                d_cfm = EO("vector", lambda e: e.tensor_copy(cfm[:], P[7][:, 0:256]),
                           waits=bk.read_wait(7), inc="dv")
                bk.read(7, "dv", d_cfm)
                d_ao = None
                for hh in range(2):
                    bank = 2 + hh
                    w = bk.write_waits(bank) + [("dv", d_cfm)]
                    for dc in range(2):
                        EO("tensor", lambda e, dc=dc, hh=hh, bank=bank: e.matmul(
                            P[bank][:, 0:512],
                            cfm[:, dc * 128:(dc + 1) * 128],
                            woT_b[:, dc * 1024 + hh * 512: dc * 1024 + (hh + 1) * 512],
                            start=(dc == 0), stop=(dc == 1)),
                            waits=w if dc == 0 else (), inc="pe" if dc == 1 else None)
                    bk.wrote(bank, em.cnt["pe"])
                    d_ao = EO("vector", lambda e, hh=hh, ci=ci, bank=bank: e.tensor_add(
                        acc[:, ci * H + hh * 512: ci * H + (hh + 1) * 512],
                        acc[:, ci * H + hh * 512: ci * H + (hh + 1) * 512],
                        P[bank][:, 0:512]), waits=bk.read_wait(bank), inc="dv")
                    bk.read(bank, "dv", d_ao)
                ao_evac[ci] = d_ao
                s_mm, d_z = emit_state(ci, d_k, a_v, a_snap, last=(ci == NCH - 1))
                if ci + 1 < NCH:
                    a_snap = EO("scalar", lambda e: e.activation(s_sb[:], P[6][:], AF.Copy),
                                waits=bk.read_wait(6) + [("pe", s_mm)], inc="ac")
                    bk.read(6, "ac", a_snap)
                    EO("scalar", lambda e: e.activation(z_bf[:], z_f[:], AF.Copy),
                       waits=[("dv", d_z)], inc="ac")
                    a_zsnap = em.cnt["ac"]

        ht_gelu = {}
        tcount = 0
        pe_group_end = {}
        for e_i in range(NE):
            for q in range(NQ):
                qi = e_i * NQ + q
                b = qi % 2
                for tg in range(NTG):
                    for f in range(FQ):
                        hb = f % 2
                        w = [wd_q[qi], ("dv", d_ownln)] + bk.write_waits(hb)
                        for k in range(KC):
                            EO("tensor", lambda e, k=k, f=f, b=b, tg=tg, hb=hb: e.matmul(
                                P[hb][:, 0:512],
                                u_bf[:, b * 8192 + k * 1024 + f * 128: b * 8192 + k * 1024 + (f + 1) * 128],
                                xnfm[:, k * TOK + tg * 512: k * TOK + (tg + 1) * 512],
                                start=(k == 0), stop=(k == KC - 1)),
                                waits=w if k == 0 else (), inc="pe" if k == KC - 1 else None)
                        bk.wrote(hb, em.cnt["pe"])
                        a_g = EO("scalar", lambda e, f=f, hb=hb: e.activation(
                            ht_sb[:, f * 512:(f + 1) * 512], P[hb][:, 0:512], AF.Gelu),
                            waits=bk.read_wait(hb), inc="ac")
                        bk.read(hb, "ac", a_g)
                        ht_gelu[f] = a_g
                    for t in range(4):
                        ys = tcount % 3
                        tcount += 1
                        banks = (2 + 2 * ys, 3 + 2 * ys)
                        ti = tg * 4 + t
                        for hh in range(2):
                            bank = banks[hh]
                            w = bk.write_waits(bank)
                            for f in range(FQ):
                                EO("tensor", lambda e, f=f, t=t, hh=hh, b=b, bank=bank: e.matmul(
                                    P[bank][:, 0:512],
                                    ht_sb[:, f * 512 + t * 128: f * 512 + (t + 1) * 128],
                                    d_bf[:, b * 8192 + f * 1024 + hh * 512: b * 8192 + f * 1024 + (hh + 1) * 512],
                                    start=(f == 0), stop=(f == FQ - 1)),
                                    waits=(w if f == 0 else []) + [("ac", ht_gelu[f])],
                                    inc="pe" if f == FQ - 1 else None)
                            bk.wrote(bank, em.cnt["pe"])
                        pe_group_end[qi] = em.cnt["pe"]
                        for hh in range(2):
                            bank = banks[hh]
                            d_y = EO("vector", lambda e, ti=ti, hh=hh, e_i=e_i, bank=bank: e.scalar_tensor_tensor(
                                acc[:, ti * H + hh * 512: ti * H + (hh + 1) * 512],
                                P[bank][:, 0:512],
                                gates[:, ti * NE + e_i: ti * NE + e_i + 1],
                                acc[:, ti * H + hh * 512: ti * H + (hh + 1) * 512],
                                ALU.mult, ALU.add),
                                waits=bk.read_wait(bank) + [("dv", d_gates), ("dv", ao_evac[ti])], inc="dv")
                            bk.read(bank, "dv", d_y)
                if qi + 2 < NE * NQ:
                    wd_q[qi + 2] = emit_wdma(qi + 2, pe_group_end[qi])

        d_all = em.cnt["dv"]
        EO("sync", lambda e: e.dma_start(out=out.rearrange("(i p) c -> p i c", p=128), in_=acc3),
           waits=[("dv", d_all)], inc="st", amt=16)
        EO("sync", lambda e: e.wait_ge(em.sem["st"], em.cnt["st"]))
        em.play()
    return nc


def kernel(**inputs):
    f32 = np.float32
    x = np.ascontiguousarray(np.asarray(inputs["L0_attn_hidden_states"], f32).reshape(16384, H))
    wq = np.asarray(inputs["L0_attn_w_q"], f32)
    wk = np.asarray(inputs["L0_attn_w_k"], f32)
    wv = np.asarray(inputs["L0_attn_w_v"], f32)
    wo = np.asarray(inputs["L0_attn_w_o"], f32)
    base = {
        "gate_g": np.asarray(inputs["L0_attn_gate"], f32).reshape(1, 1),
        "wqT": np.ascontiguousarray(wq.T),
        "wkT": np.ascontiguousarray(wk.T),
        "wvT": np.ascontiguousarray(wv.T),
        "woT": np.ascontiguousarray(wo.T),
        "memw": np.ascontiguousarray(np.asarray(inputs["L0_attn_memory"], f32)),
        "memn": np.ascontiguousarray(np.asarray(inputs["L0_attn_memory_norm"], f32)),
        "su": np.ascontiguousarray(np.asarray(inputs["L0_ffn_shared_up"], f32).reshape(H, 4096)),
        "sd": np.ascontiguousarray(np.asarray(inputs["L0_ffn_shared_down"], f32).reshape(4096, H)),
        "ru": np.ascontiguousarray(np.asarray(inputs["L0_ffn_routed_up"], f32).reshape(8 * H, 4096)),
        "rd": np.ascontiguousarray(np.asarray(inputs["L0_ffn_routed_down"], f32).reshape(8 * 4096, H)),
        "rw": np.ascontiguousarray(np.asarray(inputs["L0_ffn_router_w"], f32)),
        "maskc": np.triu(np.ones((128, 128), f32)).astype(ml_dtypes.bfloat16),
        "identc": np.eye(128, dtype=f32),
        "onesrow": np.ones((1, 128), f32),
        "onecol": np.ones((128, 1), ml_dtypes.bfloat16),
    }
    in_maps = []
    for c in range(8):
        own = np.ascontiguousarray(x[c * TOK:(c + 1) * TOK])
        if c % 2 == 1:
            pred = np.ascontiguousarray(x[(c - 1) * TOK:c * TOK])
            cp = np.ones((1, 1), f32)
        else:
            pred = own
            cp = np.zeros((1, 1), f32)
        m = dict(base)
        m["x_own"] = own
        m["x_pred"] = pred
        m["cpred"] = cp
        in_maps.append(m)

    nc = build_program()
    res = run_bass_kernel_spmd(nc, in_maps, core_ids=list(range(8)))
    outs = [np.asarray(r["out"]) for r in res.results]
    return np.concatenate(outs, axis=0).reshape(4, 4096, H)
